# version 01 (final)
# baseline (speedup 1.0000x reference)
import contextlib

import numpy as np
import concourse.bass as bass
import concourse.mybir as mybir
from concourse.bass_utils import run_bass_kernel_spmd

F32 = mybir.dt.float32
BF16 = mybir.dt.bfloat16
I8 = mybir.dt.int8
AF = mybir.ActivationFunctionType
ALU = mybir.AluOpType

C = 2048
NCT = 16
T = 2048
NCORES = 8
GROUPS = [[0, 1, 2, 3], [4, 5, 6, 7]]
LN_EPS = 1e-5


class Ctr:

    def __init__(self, sem):
        self.sem = sem
        self.val = 0

    def inc(self, n=1):
        self.val += n
        return self.val



L0F_TC = 512
L0F_NI = 64


def emit_l0ffn_convert(g, wconv, wg8s, wu8s, wd8s, wg_scr, wu_scr, wd_scr):
    g.dma_start(out=wg_scr[:, :, :], in_=wg8s[:, :, :]).then_inc(wconv.sem, 16)
    wconv.inc(16)
    g.dma_start(out=wu_scr[:, :, :], in_=wu8s[:, :, :]).then_inc(wconv.sem, 16)
    wconv.inc(16)
    g.dma_start(out=wd_scr[:, :, :, :], in_=wd8s[:, :, :, :]).then_inc(wconv.sem, 16)
    wconv.inc(16)


def emit_l0ffn_sublayer(nc, sb, xin, xout, wg_scr, wu_scr, wd_scr,
                        sg_t, su_t, sdh_t, ones128, eps_ap, ctrs):
    for chunk in range(T // L0F_TC):
        _l0ffn_chunk(nc, sb, xin, xout, wg_scr, wu_scr, wd_scr,
                     sg_t, su_t, sdh_t, ones128, eps_ap, ctrs,
                     chunk * L0F_TC)


def _l0ffn_chunk(nc, sb, xin, xout, wg_scr, wu_scr, wd_scr,
                 sg_t, su_t, sdh_t, ones128, eps_ap, ctrs, t0):
    TC = L0F_TC
    NI = L0F_NI
    xb, sqr, fuse, gsc, mb, rb = (sb["xb"], sb["sqr"], sb["fuse"], sb["gsc"],
                                  sb["mb"], sb["rb"])
    wgs, wus, wds = sb["wgs"], sb["wus"], sb["wds"]
    ps = sb["ps"]
    ld, wldA, wldB, mm, sq, st, ev, out = (
        ctrs["ld"], ctrs["wldA"], ctrs["wldB"], ctrs["mm"], ctrs["sq"],
        ctrs["st"], ctrs["ev"], ctrs["out"])

    with nc.Block() as blk:
        ld0 = ld.val
        sq0 = sq.val
        mm0 = mm.val
        st0 = st.val

        @blk.gpsimd
        def _(g):
            g.dma_start(
                out=xb[:, :].rearrange("p (ct t) -> p ct t", ct=NCT),
                in_=xin[:, t0:t0 + TC].rearrange("(ct p) t -> p ct t",
                                                 p=128),
            ).then_inc(ld.sem, 16)
            ld.inc(16)

        @blk.vector
        def _(v):
            v.wait_ge(ld.sem, ld0 + 16)
            for ct in range(NCT):
                slx = slice(ct * TC, (ct + 1) * TC)
                v.tensor_tensor(sqr[:, :TC], xb[:, slx], xb[:, slx],
                                ALU.mult).then_inc(sq.sem, 1)
                sq.inc()
                v.wait_ge(mm.sem, mm0 + ct + 1)
            v.wait_ge(mm.sem, mm0 + NCT)
            v.tensor_scalar(mb[:, :TC], ps[:, 0:TC], 1.0 / C, None, ALU.mult)
            v.tensor_tensor(rb[:, :TC], mb[:, :TC], mb[:, :TC], ALU.mult)
            v.scalar_tensor_tensor(rb[:, :TC], ps[:, 512:512 + TC], 1.0 / C,
                                   rb[:, :TC], ALU.mult,
                                   ALU.subtract).then_inc(st.sem, 1)
            st.inc()
            v.wait_ge(st.sem, st0 + 2)
            for ct in range(NCT):
                slx = slice(ct * TC, (ct + 1) * TC)
                v.tensor_tensor(xb[:, slx], xb[:, slx], mb[:, :TC],
                                ALU.subtract)
                v.tensor_tensor(xb[:, slx], xb[:, slx], rb[:, :TC],
                                ALU.mult).then_inc(ev.sem, 1)
                ev.inc()

        @blk.tensor
        def _(t):
            t.wait_ge(ld.sem, ld0 + 16)
            for ct in range(NCT):
                t.matmul(ps[:, 0:TC], ones128[:, :],
                         xb[:, ct * TC:(ct + 1) * TC],
                         start=(ct == 0), stop=(ct == NCT - 1))
            for ct in range(NCT):
                t.wait_ge(sq.sem, sq0 + ct + 1)
                ins = t.matmul(ps[:, 512:512 + TC], ones128[:, :],
                               sqr[:, :TC],
                               start=(ct == 0), stop=(ct == NCT - 1))
                ins.then_inc(mm.sem, 1)
                mm.inc()

        @blk.scalar
        def _(s):
            s.wait_ge(st.sem, st0 + 1)
            s.activation(rb[:, :TC], rb[:, :TC], AF.Ln, bias=eps_ap[:, 0:1])
            s.activation(rb[:, :TC], rb[:, :TC], AF.Exp,
                         scale=-0.5).then_inc(st.sem, 1)
            st.inc()

    if sb.get("dbg_u") is not None and t0 == 0:
        with nc.Block() as blk:
            @blk.gpsimd
            def _(g):
                g.dma_start(out=sb["dbg_u"][:, :], in_=xb[:, :]
                            ).then_inc(out.sem, 16)
                out.inc(16)
                g.wait_ge(out.sem, out.val)

    NS = NI // 2
    with nc.Block() as blk:
        wA0 = wldA.val
        mmA0 = mm.val
        stA0 = st.val
        evA0 = ev.val

        @blk.sync
        def _(s):
            for si in range(NS):
                buf = si % 2
                if si >= 2:
                    s.wait_ge(mm.sem, mmA0 + 2 * (2 * (si - 2) + 2))
                s.dma_start(out=wgs[:, buf * 4096:(buf + 1) * 4096],
                            in_=wg_scr[si, :, :]).then_inc(wldA.sem, 16)
                wldA.inc(16)
                s.dma_start(out=wus[:, buf * 4096:(buf + 1) * 4096],
                            in_=wu_scr[si, :, :]).then_inc(wldA.sem, 16)
                wldA.inc(16)
                s.wait_ge(wldA.sem, wldA.val)

        @blk.tensor
        def _(t):
            for si in range(NS):
                buf = si % 2
                t.wait_ge(wldA.sem, wA0 + (si + 1) * 32)
                for ii in range(2):
                    i = 2 * si + ii
                    pg = i % 2
                    if i >= 2:
                        t.wait_ge(ev.sem, evA0 + (i - 1))
                    gv = ps[:, (2 + pg) * 512:(3 + pg) * 512]
                    uv = ps[:, (4 + pg) * 512:(5 + pg) * 512]
                    for ct in range(NCT):
                        wsl = slice(buf * 4096 + ct * 256 + ii * 128,
                                    buf * 4096 + ct * 256 + (ii + 1) * 128)
                        ins = t.matmul(gv, wgs[:, wsl],
                                       xb[:, ct * TC:(ct + 1) * TC],
                                       start=(ct == 0), stop=(ct == NCT - 1))
                    ins.then_inc(mm.sem, 1)
                    mm.inc()
                    for ct in range(NCT):
                        wsl = slice(buf * 4096 + ct * 256 + ii * 128,
                                    buf * 4096 + ct * 256 + (ii + 1) * 128)
                        ins = t.matmul(uv, wus[:, wsl],
                                       xb[:, ct * TC:(ct + 1) * TC],
                                       start=(ct == 0), stop=(ct == NCT - 1))
                    ins.then_inc(mm.sem, 1)
                    mm.inc()

        @blk.scalar
        def _(s):
            for i in range(NI):
                pg = i % 2
                s.wait_ge(mm.sem, mmA0 + 2 * i + 1)
                if i >= 2:
                    s.wait_ge(ev.sem, evA0 + i - 1)
                gv = ps[:, (2 + pg) * 512:(3 + pg) * 512]
                s.activation(gsc[:, pg * TC:(pg + 1) * TC], gv, AF.Silu,
                             scale=sg_t[:, i:i + 1]).then_inc(st.sem, 1)
                st.inc()

        @blk.vector
        def _(v):
            v.wait_ge(ld.sem, ld.val)
            for i in range(NI):
                pg = i % 2
                v.wait_ge(st.sem, stA0 + i + 1)
                v.wait_ge(mm.sem, mmA0 + 2 * i + 2)
                uv = ps[:, (4 + pg) * 512:(5 + pg) * 512]
                v.scalar_tensor_tensor(
                    fuse[:, i * TC:(i + 1) * TC], uv, su_t[:, i:i + 1],
                    gsc[:, pg * TC:(pg + 1) * TC],
                    ALU.mult, ALU.mult).then_inc(ev.sem, 1)
                ev.inc()

    if sb.get("dbg_fuse") is not None and t0 == 0:
        with nc.Block() as blk:
            @blk.gpsimd
            def _(g):
                g.dma_start(out=sb["dbg_fuse"][:, :], in_=fuse[:, :]
                            ).then_inc(out.sem, 16)
                out.inc(16)
                g.wait_ge(out.sem, out.val)

    with nc.Block() as blk:
        wB0 = wldB.val
        mmB0 = mm.val
        evB0 = ev.val
        ldaB0 = ctrs["lda"].val
        outB0 = out.val

        @blk.sync
        def _(s):
            for opass in range(2):
                for si in range(NS):
                    buf = si % 2
                    idx = opass * NS + si
                    if idx >= 2:
                        s.wait_ge(mm.sem, mmB0 + idx - 1)
                    s.dma_start(out=wds[:, buf * 2048:(buf + 1) * 2048],
                                in_=wd_scr[opass, si, :, :]
                                ).then_inc(wldB.sem, 16)
                    wldB.inc(16)
                    s.wait_ge(wldB.sem, wldB.val)

        @blk.scalar
        def _(s):
            lda = ctrs["lda"]
            for ot in range(NCT):
                s.dma_start(out=sb["xr16"][:, ot * TC:(ot + 1) * TC],
                            in_=xin[ot * 128:(ot + 1) * 128, t0:t0 + TC]
                            ).then_inc(lda.sem, 16)
                lda.inc(16)

        @blk.tensor
        def _(t):
            for opass in range(2):
                if opass == 1:
                    t.wait_ge(ev.sem, evB0 + 8)
                for si in range(NS):
                    buf = si % 2
                    idx = opass * NS + si
                    t.wait_ge(wldB.sem, wB0 + (idx + 1) * 16)
                    for ii in range(2):
                        i = 2 * si + ii
                        for o2 in range(8):
                            wsl = slice(buf * 2048 + ii * 1024 + o2 * 128,
                                        buf * 2048 + ii * 1024 + (o2 + 1) * 128)
                            ins = t.matmul(
                                ps[:, o2 * 512:(o2 + 1) * 512],
                                wds[:, wsl], fuse[:, i * TC:(i + 1) * TC],
                                start=(i == 0), stop=(i == NI - 1))
                    ins.then_inc(mm.sem, 1)
                    mm.inc()

        @blk.vector
        def _(v):
            v.wait_ge(ctrs["lda"].sem, ldaB0 + NCT * 16)
            for opass in range(2):
                v.wait_ge(mm.sem, mmB0 + (opass + 1) * NS)
                for o2 in range(8):
                    ot = opass * 8 + o2
                    v.scalar_tensor_tensor(
                        sb["xw16"][:, ot * TC:(ot + 1) * TC],
                        ps[:, o2 * 512:(o2 + 1) * 512],
                        sdh_t[:, ot:ot + 1],
                        sb["xr16"][:, ot * TC:(ot + 1) * TC],
                        ALU.mult, ALU.add).then_inc(ev.sem, 1)
                    ev.inc()

        @blk.gpsimd
        def _(g):
            g.wait_ge(ev.sem, evB0 + NCT)
            for ot in range(NCT):
                g.dma_start(out=xout[ot * 128:(ot + 1) * 128, t0:t0 + TC],
                            in_=sb["xw16"][:, ot * TC:(ot + 1) * TC]
                            ).then_inc(out.sem, 16)
                out.inc(16)
            g.wait_ge(out.sem, out.val)


def build_l0ffn_launch(debug=False):
    nc = bass.Bass("TRN2", target_bir_lowering=False)
    xT = nc.dram_tensor("xT", [C, T], F32, kind="ExternalInput").ap()
    wg8s = nc.dram_tensor("wg8s", [32, 128, 4096], I8, kind="ExternalInput").ap()
    wu8s = nc.dram_tensor("wu8s", [32, 128, 4096], I8, kind="ExternalInput").ap()
    wd8s = nc.dram_tensor("wd8s", [2, 32, 128, 2048], I8, kind="ExternalInput").ap()
    sg = nc.dram_tensor("sg", [128, 64], F32, kind="ExternalInput").ap()
    su = nc.dram_tensor("su", [128, 64], F32, kind="ExternalInput").ap()
    sd = nc.dram_tensor("sd", [128, 16], F32, kind="ExternalInput").ap()
    yT = nc.dram_tensor("yT", [C, T], F32, kind="ExternalOutput").ap()
    wg_scr = nc.dram_tensor("wg_scr", [32, 128, 4096], BF16).ap()
    wu_scr = nc.dram_tensor("wu_scr", [32, 128, 4096], BF16).ap()
    wd_scr = nc.dram_tensor("wd_scr", [2, 32, 128, 2048], BF16).ap()

    with contextlib.ExitStack() as ctx:
        E = ctx.enter_context
        ones128 = E(nc.sbuf_tensor("ones128", [128, 128], BF16))
        eps_ap = E(nc.sbuf_tensor("eps_ap", [128, 1], F32))
        sg_t = E(nc.sbuf_tensor("sg_t", [128, 64], F32))
        su_t = E(nc.sbuf_tensor("su_t", [128, 64], F32))
        sdh_t = E(nc.sbuf_tensor("sdh_t", [128, 16], F32))
        xb = E(nc.sbuf_tensor("xb", [128, NCT * L0F_TC], BF16))
        sqr = E(nc.sbuf_tensor("sqr", [128, L0F_TC], BF16))
        fuse = E(nc.sbuf_tensor("fuse", [128, L0F_NI * L0F_TC], BF16))
        gsc = E(nc.sbuf_tensor("gsc", [128, 2 * L0F_TC], BF16))
        mb = E(nc.sbuf_tensor("mb", [128, L0F_TC], F32))
        rb = E(nc.sbuf_tensor("rb", [128, L0F_TC], F32))
        wgs = E(nc.sbuf_tensor("wgs", [128, 2 * 4096], BF16))
        wus = E(nc.sbuf_tensor("wus", [128, 2 * 4096], BF16))
        wds = E(nc.sbuf_tensor("wds", [128, 2 * 2048], BF16))
        xr16 = E(nc.sbuf_tensor("xr16", [128, NCT * L0F_TC], F32))
        xw16 = E(nc.sbuf_tensor("xw16", [128, NCT * L0F_TC], F32))
        ps = E(nc.psum_tensor("ps", [128, 4096], F32))
        s_ld = E(nc.semaphore("s_ld"))
        s_wc = E(nc.semaphore("s_wc"))
        s_wa = E(nc.semaphore("s_wa"))
        s_wb = E(nc.semaphore("s_wb"))
        s_mm = E(nc.semaphore("s_mm"))
        s_sq = E(nc.semaphore("s_sq"))
        s_st = E(nc.semaphore("s_st"))
        s_ev = E(nc.semaphore("s_ev"))
        s_out = E(nc.semaphore("s_out"))
        s_lds = E(nc.semaphore("s_lds"))
        s_lda = E(nc.semaphore("s_lda"))
        ctrs = dict(ld=Ctr(s_ld), lds=Ctr(s_lds), lda=Ctr(s_lda),
                    wconv=Ctr(s_wc), wldA=Ctr(s_wa),
                    wldB=Ctr(s_wb), mm=Ctr(s_mm), sq=Ctr(s_sq),
                    st=Ctr(s_st), ev=Ctr(s_ev), out=Ctr(s_out))
        with nc.Block() as blk:
            @blk.gpsimd
            def _(g):
                g.memset(eps_ap[:, :], LN_EPS)
                g.memset(ones128[:, :], 1.0)
                emit_l0ffn_convert(g, ctrs["wconv"], wg8s, wu8s, wd8s,
                                   wg_scr, wu_scr, wd_scr)
                g.wait_ge(ctrs["wconv"].sem, ctrs["wconv"].val)

            @blk.sync
            def _(s):
                s.dma_start(out=sg_t[:, :], in_=sg[:, :]).then_inc(s_lds, 16)
                s.dma_start(out=su_t[:, :], in_=su[:, :]).then_inc(s_lds, 16)
                s.dma_start(out=sdh_t[:, :], in_=sd[:, :]).then_inc(s_lds, 16)
                ctrs["lds"].inc(48)

            @blk.vector
            def _(v):
                v.wait_ge(s_lds, ctrs["lds"].val)
                v.tensor_scalar(sdh_t[:, :], sdh_t[:, :], 0.5, None, ALU.mult)

        sb = dict(xb=xb, sqr=sqr, fuse=fuse, gsc=gsc, mb=mb, rb=rb,
                  wgs=wgs, wus=wus, wds=wds, ps=ps,
                  xr16=xr16, xw16=xw16,
                  dbg_u=None, dbg_fuse=None)
        if debug:
            sb["dbg_u"] = nc.dram_tensor(
                "dbg_u", [128, NCT * L0F_TC], BF16, kind="ExternalOutput").ap()
            sb["dbg_fuse"] = nc.dram_tensor(
                "dbg_fuse", [128, L0F_NI * L0F_TC], BF16,
                kind="ExternalOutput").ap()
        emit_l0ffn_sublayer(nc, sb, xT, yT, wg_scr, wu_scr, wd_scr,
                            sg_t, su_t, sdh_t, ones128, eps_ap, ctrs)
    return nc



def _shard_xT(x):
    xf = np.asarray(x).reshape(NCORES, T, C)
    return [np.ascontiguousarray(xf[c].T) for c in range(NCORES)]


def _unshard_xT(shards):
    out = np.stack([np.asarray(s).T for s in shards], axis=0)
    return np.ascontiguousarray(out.reshape(2, 8192, C))


def _slab_gate(w):
    return np.ascontiguousarray(
        np.asarray(w).reshape(NCT, 128, 32, 256).transpose(2, 1, 0, 3)
        .reshape(32, 128, 4096))


def _slab_down(w):
    return np.ascontiguousarray(
        np.asarray(w).reshape(32, 2, 128, 2, 1024).transpose(3, 0, 2, 1, 4)
        .reshape(2, 32, 128, 2048))


def _pcol(v, n):
    return np.ascontiguousarray(np.asarray(v).reshape(n, 128).T)


def run_l0ffn(x, w_gate, w_up, w_down, s_gate, s_up, s_down):
    nc = build_l0ffn_launch()
    shards = _shard_xT(x)
    wg = _slab_gate(w_gate)
    wu = _slab_gate(w_up)
    wd = _slab_down(w_down)
    sgl = _pcol(s_gate, 64)
    sul = _pcol(s_up, 64)
    sdl = _pcol(s_down, 16)
    in_maps = [{"xT": shards[c], "wg8s": wg, "wu8s": wu, "wd8s": wd,
                "sg": sgl, "su": sul, "sd": sdl} for c in range(NCORES)]
    res = run_bass_kernel_spmd(nc, in_maps, core_ids=list(range(NCORES)))
    return _unshard_xT([res.results[c]["yT"] for c in range(NCORES)])



def emit_ln(nc, ctrs, *, src, dst, sqr, mb, rb, ps, ones128, eps_ap,
            tlen, gain=None, bias=None):
    mm, sq, st, ev = ctrs["mm"], ctrs["sq"], ctrs["st"], ctrs["ev"]
    ncol = tlen // 512
    with nc.Block() as blk:
        sq0, mm0, st0 = sq.val, mm.val, st.val

        @blk.vector
        def _(v):
            for ct in range(NCT):
                slx = slice(ct * tlen, (ct + 1) * tlen)
                v.tensor_tensor(sqr[:, :], src[:, slx], src[:, slx],
                                ALU.mult).then_inc(sq.sem, 1)
                sq.inc()
                v.wait_ge(mm.sem, mm0 + ct + 1)
            v.wait_ge(mm.sem, mm0 + NCT)
            v.tensor_scalar(mb[:, :], ps[:, 0:tlen], 1.0 / C, None, ALU.mult)
            v.tensor_tensor(rb[:, :], mb[:, :], mb[:, :], ALU.mult)
            v.scalar_tensor_tensor(rb[:, :], ps[:, 2048:2048 + tlen],
                                   1.0 / C, rb[:, :], ALU.mult,
                                   ALU.subtract).then_inc(st.sem, 1)
            st.inc()
            v.wait_ge(st.sem, st0 + 2)
            for ct in range(NCT):
                slx = slice(ct * tlen, (ct + 1) * tlen)
                v.tensor_tensor(dst[:, slx], src[:, slx], mb[:, :],
                                ALU.subtract)
                if gain is None:
                    v.tensor_tensor(dst[:, slx], dst[:, slx], rb[:, :],
                                    ALU.mult)
                else:
                    v.tensor_tensor(dst[:, slx], dst[:, slx], rb[:, :],
                                    ALU.mult)
                    v.tensor_scalar(dst[:, slx], dst[:, slx],
                                    gain[:, ct:ct + 1], bias[:, ct:ct + 1],
                                    ALU.mult, ALU.add)

        @blk.tensor
        def _(t):
            for ct in range(NCT):
                for j in range(ncol):
                    c0 = j * 512
                    t.matmul(ps[:, c0:c0 + 512], ones128[:, :],
                             src[:, ct * tlen + c0:ct * tlen + c0 + 512],
                             start=(ct == 0), stop=(ct == NCT - 1))
            for ct in range(NCT):
                t.wait_ge(sq.sem, sq0 + ct + 1)
                for j in range(ncol):
                    c0 = j * 512
                    ins = t.matmul(ps[:, 2048 + c0:2048 + c0 + 512],
                                   ones128[:, :], sqr[:, c0:c0 + 512],
                                   start=(ct == 0), stop=(ct == NCT - 1))
                ins.then_inc(mm.sem, 1)
                mm.inc()

        @blk.scalar
        def _(s):
            s.wait_ge(st.sem, st0 + 1)
            s.activation(rb[:, :], rb[:, :], AF.Ln, bias=eps_ap[:, 0:1])
            s.activation(rb[:, :], rb[:, :], AF.Exp,
                         scale=-0.5).then_inc(st.sem, 1)
            st.inc()


def _reshape_free(ap2d, dims):
    names = " ".join(f"d{i}" for i in range(len(dims)))
    kw = {f"d{i}": d for i, d in enumerate(dims[:-1])}
    return ap2d.rearrange(f"p ({names}) -> p {names}", **kw)


def emit_stage16(nc, ctrs, *, w_r=None, w_slab=None, rhs, wbuf, out_nt, ps,
                 tlen, epi_scalar=None, epi_vector=None,
                 export_scalar=None):
    mm, st, ev, wld = ctrs["mm"], ctrs["st"], ctrs["ev"], ctrs["wld"]
    nslab = out_nt // 4
    nfc = tlen // 512
    if w_slab is None:
        def w_slab(s_eng, si, dst2d):
            s_eng.dma_start(out=_reshape_free(dst2d, [NCT, 512]),
                            in_=w_r[:, :, si * 512:(si + 1) * 512]
                            ).then_inc(wld.sem, 16)
            return 1
    with nc.Block() as blk:
        mm0, st0, ev0, w0 = mm.val, st.val, ev.val, wld.val
        guard = ev if epi_vector is not None else st
        g0 = ev0 if epi_vector is not None else st0

        wld_need = {}

        @blk.sync
        def _(s):
            for si in range(nslab):
                buf = si % 2
                if si >= 2:
                    s.wait_ge(mm.sem, mm0 + 4 * (si - 1))
                nd = w_slab(s, si, wbuf[:, buf * 8192:(buf + 1) * 8192])
                wld.inc(16 * nd)
                wld_need[si] = wld.val
                s.wait_ge(wld.sem, wld_need[si])

        @blk.tensor
        def _(t):
            for ot in range(out_nt):
                si, orel = ot // 4, ot % 4
                buf = si % 2
                pp = ot % 2
                t.wait_ge(wld.sem, wld_need[si])
                if ot >= 2:
                    t.wait_ge(guard.sem, g0 + ot - 1)
                for fc in range(nfc):
                    for ct in range(NCT):
                        wsl = slice(buf * 8192 + ct * 512 + orel * 128,
                                    buf * 8192 + ct * 512 + (orel + 1) * 128)
                        ins = t.matmul(
                            ps[:, pp * tlen + fc * 512:pp * tlen + (fc + 1) * 512],
                            wbuf[:, wsl],
                            rhs[:, ct * tlen + fc * 512:ct * tlen + (fc + 1) * 512],
                            start=(ct == 0), stop=(ct == NCT - 1))
                ins.then_inc(mm.sem, 1)
                mm.inc()

        if export_scalar is not None:
            out2 = ctrs["out2"]
            out20 = out2.val

            @blk.scalar
            def _(s):
                for ot in range(out_nt):
                    s.wait_ge(ev.sem, ev0 + ot + 1)
                    export_scalar(s, ot)

        if epi_scalar is not None:
            @blk.scalar
            def _(s):
                for ot in range(out_nt):
                    pp = ot % 2
                    s.wait_ge(mm.sem, mm0 + ot + 1)
                    if epi_vector is not None and ot >= 2:
                        s.wait_ge(ev.sem, ev0 + ot - 1)
                    ins = epi_scalar(s, ot, ps[:, pp * tlen:(pp + 1) * tlen])
                    ins.then_inc(st.sem, 1)
                    st.inc()

        if epi_vector is not None:
            @blk.vector
            def _(v):
                for ot in range(out_nt):
                    pp = ot % 2
                    v.wait_ge(mm.sem, mm0 + ot + 1)
                    if epi_scalar is not None:
                        v.wait_ge(st.sem, st0 + ot + 1)
                    if export_scalar is not None and ot >= 2:
                        v.wait_ge(ctrs["out2"].sem, out20 + (ot - 1) * 16)
                    ins = epi_vector(v, ot, ps[:, pp * tlen:(pp + 1) * tlen])
                    ins.then_inc(ev.sem, 1)
                    ev.inc()




PI = float(np.pi)
L1F_TC = 1024


def _emit_range_reduce_row(v, row, rtmp, n):
    for _ in range(2):
        v.tensor_scalar(rtmp[0:1, 0:n], row[0:1, 0:n], PI, None, ALU.is_gt)
        v.scalar_tensor_tensor(row[0:1, 0:n], rtmp[0:1, 0:n], -2 * PI,
                               row[0:1, 0:n], ALU.mult, ALU.add)
        v.tensor_scalar(rtmp[0:1, 0:n], row[0:1, 0:n], -PI, None, ALU.is_lt)
        v.scalar_tensor_tensor(row[0:1, 0:n], rtmp[0:1, 0:n], 2 * PI,
                               row[0:1, 0:n], ALU.mult, ALU.add)


def emit_l1ffn_precompute(nc, ctx2, sb, ctrs, hbm):
    ld, wconv, mm, st, ev, cc = (ctrs["ld"], ctrs["wconv"], ctrs["mm"],
                                 ctrs["st"], ctrs["ev"], ctrs["cc"])
    ps = sb["ps"]
    E = ctx2.enter_context
    ewT = E(nc.sbuf_tensor("ewT", [128, 64 * 256], BF16))
    pwsl = E(nc.sbuf_tensor("pwsl", [128, 2 * 2048], BF16))
    msb = E(nc.sbuf_tensor("msb", [128, 2 * 2048], BF16))
    modb = E(nc.sbuf_tensor("modb", [128, 2 * 2048], F32))
    pwrows = E(nc.sbuf_tensor("pwrows", [128, 4 * 2048], BF16))
    twc1 = E(nc.sbuf_tensor("twc1", [128, 2048], F32))
    twc2 = E(nc.sbuf_tensor("twc2", [128, 2048], F32))
    ebcol = E(nc.sbuf_tensor("ebcol", [128, 128], BF16))
    phrow = E(nc.sbuf_tensor("phrow", [1, 2048], F32))
    modrow = E(nc.sbuf_tensor("modrow", [1, 8192], F32))
    pbrow = E(nc.sbuf_tensor("pbrow", [1, 2048], F32))
    bcrow = E(nc.sbuf_tensor("bcrow", [1, 4096], F32))
    rtmp = E(nc.sbuf_tensor("rtmp", [1, 2048], F32))
    rtmp2 = E(nc.sbuf_tensor("rtmp2", [1, 2048], F32))
    ones1 = E(nc.sbuf_tensor("ones1", [1, 128], F32))
    bmrow = E(nc.sbuf_tensor("bmrow", [1, 256], F32))

    with nc.Block() as blk:
        @blk.gpsimd
        def _(g):
            g.memset(ones1[:, :], 1.0)
            g.dma_start(out=hbm["ampw_scr"][:, :],
                        in_=hbm["amp_w"][:, :]).then_inc(wconv.sem, 16)
            g.dma_start(out=hbm["phw_scr"][:, :],
                        in_=hbm["ph_w"][:, :]).then_inc(wconv.sem, 16)
            for l in range(2):
                g.dma_start(out=hbm["entpw_scr"][l, :, :],
                            in_=hbm["ent_pw"][l, :, :]).then_inc(wconv.sem, 16)
            wconv.inc(64)
            g.dma_start(out=ebcol[:, :], in_=hbm["ent_bcol"][:, :]
                        ).then_inc(ld.sem, 16)
            ld.inc(16)
            g.wait_ge(wconv.sem, wconv.val)
            g.wait_ge(ld.sem, ld.val)

    for l in range(2):
        with nc.Block() as blk:
            ld0, mm0 = ld.val, mm.val

            @blk.gpsimd
            def _(g):
                for q in range(4):
                    g.dma_start(
                        out=ewT[:, q * 16 * 256:(q + 1) * 16 * 256]
                        .rearrange("p (it r) -> p it r", it=16),
                        in_=hbm["ent_wT_sl"][l, q * 2048:(q + 1) * 2048, :]
                        .rearrange("(it p) r -> p it r", p=128),
                    ).then_inc(ld.sem, 16)
                    ld.inc(16)

            @blk.sync
            def _(s):
                wld = ctrs["wld"]
                for it in range(64):
                    buf = it % 2
                    if it >= 2:
                        s.wait_ge(mm.sem, mm0 + it - 1)
                    s.dma_start(out=pwsl[:, buf * 2048:(buf + 1) * 2048],
                                in_=hbm["entpw_scr"][l, it * 128:(it + 1) * 128, :]
                                ).then_inc(wld.sem, 16)
                    wld.inc(16)
                    s.wait_ge(wld.sem, wld.val)

            @blk.tensor
            def _(t):
                wld = ctrs["wld"]
                wld0 = wld.val - 64 * 16
                t.wait_ge(ld.sem, ld0 + 64)
                for it in range(64):
                    buf = it % 2
                    t.wait_ge(wld.sem, wld0 + (it + 1) * 16)
                    ins = None
                    for rt in range(2):
                        for ocb in range(4):
                            bank = rt * 4 + ocb
                            ins = t.matmul(
                                ps[:, bank * 512:(bank + 1) * 512],
                                ewT[:, it * 256 + rt * 128:
                                    it * 256 + (rt + 1) * 128],
                                pwsl[:, buf * 2048 + ocb * 512:
                                     buf * 2048 + (ocb + 1) * 512],
                                start=(it == 0), stop=(it == 63))
                    ins.then_inc(mm.sem, 1)
                    mm.inc()

            @blk.vector
            def _(v):
                v.wait_ge(mm.sem, mm0 + 64)
                ins = None
                for rt in range(2):
                    for ocb in range(4):
                        bank = rt * 4 + ocb
                        ins = v.tensor_copy(
                            msb[:, rt * 2048 + ocb * 512:
                                rt * 2048 + (ocb + 1) * 512],
                            ps[:, bank * 512:(bank + 1) * 512])
                ins.then_inc(ev.sem, 1)
                ev.inc()

            @blk.gpsimd
            def _(g):
                g.wait_ge(ev.sem, ev.val)
                for rt in range(2):
                    g.dma_start(
                        out=hbm["Msh"][l * 256 + rt * 128:
                                       l * 256 + (rt + 1) * 128, :],
                        in_=msb[:, rt * 2048:(rt + 1) * 2048],
                    ).then_inc(ld.sem, 16)
                    ld.inc(16)
                g.wait_ge(ld.sem, ld.val)

        with nc.Block() as blk:
            ld0, mm0, ev0 = ld.val, mm.val, ev.val

            @blk.gpsimd
            def _(g):
                for q in range(4):
                    g.dma_start(
                        out=ewT[:, q * 16 * 256:(q + 1) * 16 * 256]
                        .rearrange("p (it r) -> p it r", it=16),
                        in_=hbm["ent_pw_osl"][l, q * 2048:(q + 1) * 2048, :]
                        .rearrange("(it p) r -> p it r", p=128),
                    ).then_inc(ld.sem, 16)
                    ld.inc(16)
                g.dma_start(out=bmrow[0:1, 0:256],
                            in_=hbm["ent_pb_sl"][l:l + 1, :]
                            ).then_inc(ld.sem, 16)
                ld.inc(16)

            @blk.tensor
            def _(t):
                t.wait_ge(ld.sem, ld0 + 80)
                ins = None
                for it in range(64):
                    ins = t.matmul(ps[0:1, 0:256],
                                   ebcol[:, l * 64 + it:l * 64 + it + 1],
                                   ewT[:, it * 256:it * 256 + 256],
                                   start=(it == 0), stop=(it == 63))
                ins.then_inc(mm.sem, 1)
                mm.inc()

            @blk.vector
            def _(v):
                v.wait_ge(mm.sem, mm0 + 1)
                v.tensor_tensor(bmrow[0:1, 0:256], ps[0:1, 0:256],
                                bmrow[0:1, 0:256], ALU.add).then_inc(
                                    ev.sem, 1)
                ev.inc()

            @blk.gpsimd
            def _(g):
                g.wait_ge(ev.sem, ev0 + 1)
                g.dma_start(out=hbm["bsh"][0:1, l * 256:(l + 1) * 256],
                            in_=bmrow[0:1, 0:256]).then_inc(ld.sem, 16)
                ld.inc(16)
                g.wait_ge(ld.sem, ld.val)

    for l in range(2):
        for p in range(2):
            off = (l * 2 + p) * 2048
            with nc.Block() as blk:
                ld0, st0, ev0 = ld.val, st.val, ev.val

                @blk.gpsimd
                def _(g):
                    g.dma_start(out=phrow[0:1, :],
                                in_=hbm["phase"][l, p:p + 1, :]
                                ).then_inc(ld.sem, 16)
                    ld.inc(16)

                @blk.vector
                def _(v):
                    v.wait_ge(ld.sem, ld0 + 16)
                    _emit_range_reduce_row(v, phrow, rtmp, 2048)
                    v.tensor_copy(rtmp[0:1, :], phrow[0:1, :]).then_inc(
                        ev.sem, 1)
                    ev.inc()

                @blk.scalar
                def _(s):
                    s.wait_ge(ev.sem, ev0 + 1)
                    s.activation(rtmp2[0:1, :], phrow[0:1, :], AF.Sin)
                    s.activation(rtmp[0:1, :], rtmp[0:1, :], AF.Abs)
                    s.activation(rtmp[0:1, :], rtmp[0:1, :], AF.Sin,
                                 scale=-1.0, bias=sb["halfpi"][0:1, 0:1]
                                 ).then_inc(st.sem, 1)
                    st.inc()

                @blk.vector
                def _(v):
                    v.wait_ge(st.sem, st0 + 1)
                    v.tensor_tensor(modrow[0:1, off:off + 2048],
                                    rtmp[0:1, :], rtmp2[0:1, :], ALU.add)

    for l in range(2):
        with nc.Block() as blk:
            ld0, mm0, ev0 = ld.val, mm.val, ev.val

            @blk.tensor
            def _(t):
                ins = None
                for p in range(2):
                    off = (l * 2 + p) * 2048
                    for j in range(4):
                        ins = t.matmul(
                            ps[:, p * 2048 + j * 512:p * 2048 + (j + 1) * 512],
                            ones1[0:1, :],
                            modrow[0:1, off + j * 512:off + (j + 1) * 512],
                            start=True, stop=True)
                ins.then_inc(mm.sem, 1)
                mm.inc()

            @blk.vector
            def _(v):
                v.wait_ge(mm.sem, mm0 + 1)
                v.tensor_copy(modb[:, 0:2048], ps[:, 0:2048])
                v.tensor_copy(modb[:, 2048:4096], ps[:, 2048:4096]
                              ).then_inc(ev.sem, 1)
                ev.inc()

            @blk.gpsimd
            def _(g):
                for p in range(2):
                    g.dma_start(
                        out=pwrows[:, p * 2 * 2048:(p + 1) * 2 * 2048]
                        .rearrange("q (rt o) -> q rt o", rt=2),
                        in_=hbm["path_w_sl"][l, p, :, :]
                        .rearrange("(rt q) o -> q rt o", q=128),
                    ).then_inc(ld.sem, 16)
                    ld.inc(16)

            @blk.vector
            def _(v):
                v.wait_ge(ld.sem, ld0 + 32)
                v.wait_ge(ev.sem, ev0 + 1)
                ins = None
                for rt in range(2):
                    v.tensor_tensor(twc1[:, :],
                                    pwrows[:, rt * 2048:(rt + 1) * 2048],
                                    modb[:, 0:2048], ALU.mult)
                    v.tensor_tensor(twc2[:, :],
                                    pwrows[:, (2 + rt) * 2048:(3 + rt) * 2048],
                                    modb[:, 2048:4096], ALU.mult)
                    ins = v.tensor_tensor(msb[:, rt * 2048:(rt + 1) * 2048],
                                          twc1[:, :], twc2[:, :], ALU.add)
                ins.then_inc(ev.sem, 1)
                ev.inc()

            @blk.gpsimd
            def _(g):
                g.wait_ge(ev.sem, ev0 + 2)
                for rt in range(2):
                    g.dma_start(
                        out=hbm["Wcsh"][l * 256 + rt * 128:
                                        l * 256 + (rt + 1) * 128, :],
                        in_=msb[:, rt * 2048:(rt + 1) * 2048],
                    ).then_inc(ld.sem, 16)
                    ld.inc(16)
                g.wait_ge(ld.sem, ld.val)

    with nc.Block() as blk:
        ld0, ev0 = ld.val, ev.val

        @blk.gpsimd
        def _(g):
            for l in range(2):
                for p in range(2):
                    g.dma_start(out=pbrow[0:1, :],
                                in_=hbm["path_b"][l, p:p + 1, :]
                                ).then_inc(ld.sem, 16)
                    ld.inc(16)
                    g.wait_ge(ev.sem, ev0 + l * 2 + p + 1)

        @blk.vector
        def _(v):
            for l in range(2):
                for p in range(2):
                    off = (l * 2 + p) * 2048
                    v.wait_ge(ld.sem, ld0 + (l * 2 + p + 1) * 16)
                    if p == 0:
                        v.tensor_tensor(bcrow[0:1, l * 2048:(l + 1) * 2048],
                                        pbrow[0:1, :],
                                        modrow[0:1, off:off + 2048],
                                        ALU.mult).then_inc(ev.sem, 1)
                    else:
                        v.tensor_tensor(rtmp[0:1, :], pbrow[0:1, :],
                                        modrow[0:1, off:off + 2048],
                                        ALU.mult)
                        v.tensor_tensor(bcrow[0:1, l * 2048:(l + 1) * 2048],
                                        bcrow[0:1, l * 2048:(l + 1) * 2048],
                                        rtmp[0:1, :], ALU.add).then_inc(
                                            ev.sem, 1)
                    ev.inc()

        @blk.gpsimd
        def _(g):
            g.wait_ge(ev.sem, ev.val)
            g.dma_start(out=hbm["bc_scr"][:, :],
                        in_=bcrow[0:1, 0:4096]).then_inc(ld.sem, 16)
            ld.inc(16)
            g.wait_ge(ld.sem, ld.val)

    with nc.Block() as blk:
        @blk.gpsimd
        def _(g):
            for k, (name_in, name_out) in enumerate(
                    (("Msh", "Mg"), ("Wcsh", "Wcg"), ("bsh", "bg"))):
                g.collective_compute(
                    "AllGather", ALU.bypass,
                    replica_groups=[list(range(8))],
                    ins=[hbm[name_in][:, :]], outs=[hbm[name_out][:, :]],
                ).then_inc(sb["ccsems"][k].sem, 1)
            for k in range(3):
                g.wait_ge(sb["ccsems"][k].sem, 1)

    with nc.Block() as blk:
        @blk.gpsimd
        def _(g):
            for gsrc, gdst in (("Mg", "M_scr"), ("Wcg", "Wc_scr")):
                for l in range(2):
                    g.dma_start(
                        out=hbm[gdst][l].rearrange("(a b p) o -> a b p o",
                                                   a=8, b=2),
                        in_=hbm[gsrc].rearrange("(a l b p) o -> l a b p o",
                                                a=8, l=2, b=2)[l],
                    ).then_inc(ld.sem, 16)
                    ld.inc(16)
            g.wait_ge(ld.sem, ld.val)

    with nc.Block() as blk:
        @blk.gpsimd
        def _(g):
            for l in range(2):
                for ot in range(NCT):
                    a, b = ot // 2, ot % 2
                    g.dma_start(
                        out=sb["bM_t"][:, l * 16 + ot:l * 16 + ot + 1],
                        in_=hbm["bg"][a:a + 1,
                                      l * 256 + b * 128:l * 256 + (b + 1) * 128]
                        .rearrange("q (p c) -> (q p) c", c=1),
                    ).then_inc(ld.sem, 16)
                    ld.inc(16)
                    g.dma_start(
                        out=sb["bc_t"][:, l * 16 + ot:l * 16 + ot + 1],
                        in_=hbm["bc_scr"][0:1, l * 2048 + ot * 128:
                                          l * 2048 + (ot + 1) * 128]
                        .rearrange("q (p c) -> (q p) c", c=1),
                    ).then_inc(ld.sem, 16)
                    ld.inc(16)
            g.wait_ge(ld.sem, ld.val)


def _gathered_slab_emitter(w4, ctrs):
    def w_slab(s_eng, si, dst2d):
        d4 = _reshape_free(dst2d, [8, 2, 512])
        for b in range(2):
            s_eng.dma_start(out=d4[:, :, b, :],
                            in_=w4[:, :, b, si * 512:(si + 1) * 512]
                            ).then_inc(ctrs["wld"].sem, 16)
        return 2
    return w_slab


def emit_l1ffn_sublayer(nc, sb, ctrs, hbm, xin, xout):
    TC = L1F_TC
    U, A = sb["U"], sb["A"]
    tmp1, tmp2 = sb["tmp1"], sb["tmp2"]
    ps = sb["ps"]
    ld, mm, st, ev, out = (ctrs["ld"], ctrs["mm"], ctrs["st"], ctrs["ev"],
                           ctrs["out"])
    Mg_r = hbm["M_scr"].rearrange("l (ct p) o -> l p ct o", p=128)
    Wc_r = hbm["Wc_scr"].rearrange("l (ct p) o -> l p ct o", p=128)
    ampw_r = hbm["ampw_scr"].rearrange("(ct p) o -> p ct o", p=128)
    phw_r = hbm["phw_scr"].rearrange("(ct p) o -> p ct o", p=128)

    for chunk in range(T // TC):
        t0 = chunk * TC

        with nc.Block() as blk:
            ld0 = ld.val

            @blk.gpsimd
            def _(g):
                g.dma_start(
                    out=U[:, :].rearrange("p (ct t) -> p ct t", ct=NCT),
                    in_=xin[:, t0:t0 + TC].rearrange("(ct p) t -> p ct t",
                                                     p=128),
                ).then_inc(ld.sem, 16)
                ld.inc(16)
                g.wait_ge(ld.sem, ld0 + 16)

        emit_ln(nc, ctrs, src=U, dst=U, sqr=sb["sqr"], mb=sb["mb"],
                rb=sb["rb"], ps=ps, ones128=sb["ones128"],
                eps_ap=sb["eps_ap"], tlen=TC)

        if "dbg_u" in hbm and chunk == 0:
            with nc.Block() as blk:
                @blk.gpsimd
                def _(g):
                    g.dma_start(out=hbm["dbg_u"][:, :], in_=U[:, :]
                                ).then_inc(ctrs["ld"].sem, 16)
                    ctrs["ld"].inc(16)
                    g.wait_ge(ctrs["ld"].sem, ctrs["ld"].val)

        def ph_epi_s(s, ot, psv):
            return s.activation(A[:, ot * TC:(ot + 1) * TC], psv,
                                AF.Tanh, bias=sb["phb_t"][:, ot:ot + 1])

        emit_stage16(nc, ctrs, w_r=phw_r, rhs=U, wbuf=sb["wbuf"],
                     out_nt=NCT, ps=ps, tlen=TC, epi_scalar=ph_epi_s,
                     epi_vector=None)

        with nc.Block() as blk:
            st0, ev0 = st.val, ev.val

            @blk.scalar
            def _(s):
                for ot in range(NCT):
                    pp = ot % 2
                    if ot >= 2:
                        s.wait_ge(ev.sem, ev0 + ot - 1)
                    s.activation(tmp1[:, pp * TC:(pp + 1) * TC],
                                 A[:, ot * TC:(ot + 1) * TC], AF.Sin,
                                 scale=PI)
                    s.activation(tmp2[:, pp * TC:(pp + 1) * TC],
                                 A[:, ot * TC:(ot + 1) * TC], AF.Abs)
                    s.activation(tmp2[:, pp * TC:(pp + 1) * TC],
                                 tmp2[:, pp * TC:(pp + 1) * TC], AF.Sin,
                                 scale=-PI, bias=sb["halfpi"][:, 0:1]
                                 ).then_inc(st.sem, 1)
                    st.inc()

            @blk.vector
            def _(v):
                for ot in range(NCT):
                    pp = ot % 2
                    v.wait_ge(st.sem, st0 + ot + 1)
                    v.tensor_tensor(A[:, ot * TC:(ot + 1) * TC],
                                    tmp1[:, pp * TC:(pp + 1) * TC],
                                    tmp2[:, pp * TC:(pp + 1) * TC],
                                    ALU.add).then_inc(ev.sem, 1)
                    ev.inc()

        def amp_epi_s(s, ot, psv):
            pp = ot % 2
            return s.activation(tmp1[:, pp * TC:(pp + 1) * TC], psv,
                                AF.Sigmoid, bias=sb["ampb_t"][:, ot:ot + 1])

        def amp_epi_v(v, ot, psv):
            pp = ot % 2
            return v.tensor_tensor(A[:, ot * TC:(ot + 1) * TC],
                                   A[:, ot * TC:(ot + 1) * TC],
                                   tmp1[:, pp * TC:(pp + 1) * TC], ALU.mult)

        emit_stage16(nc, ctrs, w_r=ampw_r, rhs=U, wbuf=sb["wbuf"],
                     out_nt=NCT, ps=ps, tlen=TC, epi_scalar=amp_epi_s,
                     epi_vector=amp_epi_v)

        if "dbg_h" in hbm and chunk == 0:
            with nc.Block() as blk:
                @blk.gpsimd
                def _(g):
                    g.dma_start(out=hbm["dbg_h"][:, :], in_=A[:, :]
                                ).then_inc(ctrs["ld"].sem, 16)
                    ctrs["ld"].inc(16)
                    g.wait_ge(ctrs["ld"].sem, ctrs["ld"].val)

        for l in range(2):
            def ent_epi_v(v, ot, psv, l=l):
                return v.scalar_tensor_tensor(
                    U[:, ot * TC:(ot + 1) * TC], psv,
                    sb["bM_t"][:, l * 16 + ot:l * 16 + ot + 1],
                    A[:, ot * TC:(ot + 1) * TC], ALU.add, ALU.add)

            emit_stage16(nc, ctrs, w_r=Mg_r[l], rhs=A, wbuf=sb["wbuf"],
                         out_nt=NCT, ps=ps, tlen=TC, epi_vector=ent_epi_v)
            emit_ln(nc, ctrs, src=U, dst=U, sqr=sb["sqr"], mb=sb["mb"],
                    rb=sb["rb"], ps=ps, ones128=sb["ones128"],
                    eps_ap=sb["eps_ap"], tlen=TC,
                    gain=sb["ln3g_t"][:, l * 16:(l + 1) * 16],
                    bias=sb["ln3b_t"][:, l * 16:(l + 1) * 16])

            def path_epi_v(v, ot, psv, l=l):
                return v.scalar_tensor_tensor(
                    A[:, ot * TC:(ot + 1) * TC], psv,
                    sb["bc_t"][:, l * 16 + ot:l * 16 + ot + 1],
                    U[:, ot * TC:(ot + 1) * TC], ALU.add, ALU.add)

            emit_stage16(nc, ctrs, w_r=Wc_r[l], rhs=U, wbuf=sb["wbuf"],
                         out_nt=NCT, ps=ps, tlen=TC, epi_vector=path_epi_v)
            emit_ln(nc, ctrs, src=A, dst=A, sqr=sb["sqr"], mb=sb["mb"],
                    rb=sb["rb"], ps=ps, ones128=sb["ones128"],
                    eps_ap=sb["eps_ap"], tlen=TC,
                    gain=sb["ln4g_t"][:, l * 16:(l + 1) * 16],
                    bias=sb["ln4b_t"][:, l * 16:(l + 1) * 16])

        if "dbg_hf" in hbm and chunk == 0:
            with nc.Block() as blk:
                @blk.gpsimd
                def _(g):
                    g.dma_start(out=hbm["dbg_hf"][:, :], in_=A[:, :]
                                ).then_inc(ctrs["ld"].sem, 16)
                    ctrs["ld"].inc(16)
                    g.wait_ge(ctrs["ld"].sem, ctrs["ld"].val)

        xrw = sb["xrw"]
        with nc.Block() as blk:
            lds = ctrs["lds"]
            lds0, ev0, out0 = lds.val, ev.val, out.val

            @blk.sync
            def _(s):
                for grp in range(2):
                    if grp == 1:
                        s.wait_ge(out.sem, out0 + 8 * 16)
                    for j in range(8):
                        ot = grp * 8 + j
                        s.dma_start(
                            out=xrw[:, j * TC:(j + 1) * TC],
                            in_=xin[ot * 128:(ot + 1) * 128, t0:t0 + TC],
                        ).then_inc(lds.sem, 16)
                        lds.inc(16)

            @blk.vector
            def _(v):
                for grp in range(2):
                    v.wait_ge(lds.sem, lds0 + (grp + 1) * 8 * 16)
                    for j in range(8):
                        ot = grp * 8 + j
                        v.scalar_tensor_tensor(
                            xrw[:, j * TC:(j + 1) * TC],
                            A[:, ot * TC:(ot + 1) * TC], 0.5,
                            xrw[:, j * TC:(j + 1) * TC],
                            ALU.mult, ALU.add).then_inc(ev.sem, 1)
                        ev.inc()

            @blk.gpsimd
            def _(g):
                for grp in range(2):
                    g.wait_ge(ev.sem, ev0 + (grp + 1) * 8)
                    for j in range(8):
                        ot = grp * 8 + j
                        g.dma_start(
                            out=xout[ot * 128:(ot + 1) * 128, t0:t0 + TC],
                            in_=xrw[:, j * TC:(j + 1) * TC],
                        ).then_inc(out.sem, 16)
                        out.inc(16)
                g.wait_ge(out.sem, out.val)


def build_l1ffn_launch(debug=False):
    nc = bass.Bass("TRN2", target_bir_lowering=False)
    hbm = {}
    hbm["xT"] = nc.dram_tensor("xT", [C, T], F32, kind="ExternalInput").ap()
    for name, shape in [("amp_w", [C, C]), ("ph_w", [C, C]),
                        ("ent_wT_sl", [2, 8192, 256]),
                        ("ent_pw", [2, 8192, C]),
                        ("ent_pw_osl", [2, 8192, 256]),
                        ("path_w_sl", [2, 2, 256, C]),
                        ("phase", [2, 2, C]),
                        ("path_b", [2, 2, C]), ("ent_pb_sl", [2, 256]),
                        ("ent_bcol", [128, 128]),
                        ("ampb", [128, 16]), ("phb", [128, 16]),
                        ("ln3g", [128, 32]), ("ln3b", [128, 32]),
                        ("ln4g", [128, 32]), ("ln4b", [128, 32])]:
        hbm[name] = nc.dram_tensor(name, shape, F32, kind="ExternalInput").ap()
    hbm["yT"] = nc.dram_tensor("yT", [C, T], F32, kind="ExternalOutput").ap()
    if debug:
        hbm["dbg_Mg"] = nc.dram_tensor("dbg_Mg", [4096, C], BF16,
                                       kind="ExternalOutput").ap()
        hbm["dbg_Wcg"] = nc.dram_tensor("dbg_Wcg", [4096, C], BF16,
                                        kind="ExternalOutput").ap()
        hbm["dbg_bg"] = nc.dram_tensor("dbg_bg", [8, 512], F32,
                                       kind="ExternalOutput").ap()
        hbm["dbg_bc"] = nc.dram_tensor("dbg_bc", [1, 2 * C], F32,
                                       kind="ExternalOutput").ap()
        hbm["dbg_h"] = nc.dram_tensor("dbg_h", [128, NCT * L1F_TC], BF16,
                                      kind="ExternalOutput").ap()
        hbm["dbg_u"] = nc.dram_tensor("dbg_u", [128, NCT * L1F_TC], BF16,
                                      kind="ExternalOutput").ap()
        hbm["dbg_hf"] = nc.dram_tensor("dbg_hf", [128, NCT * L1F_TC], BF16,
                                       kind="ExternalOutput").ap()
    hbm["ampw_scr"] = nc.dram_tensor("ampw_scr", [C, C], BF16).ap()
    hbm["phw_scr"] = nc.dram_tensor("phw_scr", [C, C], BF16).ap()
    hbm["entpw_scr"] = nc.dram_tensor("entpw_scr", [2, 8192, C], BF16).ap()
    hbm["Msh"] = nc.dram_tensor("Msh", [512, C], BF16).ap()
    hbm["Wcsh"] = nc.dram_tensor("Wcsh", [512, C], BF16).ap()
    hbm["bsh"] = nc.dram_tensor("bsh", [1, 512], F32).ap()
    hbm["Mg"] = nc.dram_tensor("Mg", [4096, C], BF16).ap()
    hbm["M_scr"] = nc.dram_tensor("M_scr", [2, C, C], BF16).ap()
    hbm["Wc_scr"] = nc.dram_tensor("Wc_scr", [2, C, C], BF16).ap()
    hbm["Wcg"] = nc.dram_tensor("Wcg", [4096, C], BF16).ap()
    hbm["bg"] = nc.dram_tensor("bg", [8, 512], F32).ap()
    hbm["bc_scr"] = nc.dram_tensor("bc_scr", [1, 2 * C], F32).ap()

    with contextlib.ExitStack() as ctx:
        E = ctx.enter_context
        ones128 = E(nc.sbuf_tensor("ones128", [128, 128], BF16))
        eps_ap = E(nc.sbuf_tensor("eps_ap", [128, 1], F32))
        halfpi = E(nc.sbuf_tensor("halfpi", [128, 1], F32))
        bM_t = E(nc.sbuf_tensor("bM_t", [128, 32], F32))
        bc_t = E(nc.sbuf_tensor("bc_t", [128, 32], F32))
        ampb_t = E(nc.sbuf_tensor("ampb_t", [128, 16], F32))
        phb_t = E(nc.sbuf_tensor("phb_t", [128, 16], F32))
        ln3g_t = E(nc.sbuf_tensor("ln3g_t", [128, 32], F32))
        ln3b_t = E(nc.sbuf_tensor("ln3b_t", [128, 32], F32))
        ln4g_t = E(nc.sbuf_tensor("ln4g_t", [128, 32], F32))
        ln4b_t = E(nc.sbuf_tensor("ln4b_t", [128, 32], F32))
        ps = E(nc.psum_tensor("ps", [128, 4096], F32))
        sems = {}
        for name in ["ld", "lds", "lda", "wconv", "wld", "mm", "sq", "st",
                     "ev", "out", "out2", "cc"]:
            sems[name] = Ctr(E(nc.semaphore("s_" + name)))
        ccsems = [Ctr(E(nc.semaphore(f"s_cc{k}"))) for k in range(3)]
        ctrs = sems
        sb = dict(ones128=ones128, eps_ap=eps_ap, halfpi=halfpi, ps=ps,
                  ccsems=ccsems,
                  bM_t=bM_t, bc_t=bc_t, ampb_t=ampb_t, phb_t=phb_t,
                  ln3g_t=ln3g_t, ln3b_t=ln3b_t, ln4g_t=ln4g_t, ln4b_t=ln4b_t)

        with nc.Block() as blk:
            @blk.gpsimd
            def _(g):
                g.memset(eps_ap[:, :], LN_EPS)
                g.memset(halfpi[:, :], PI / 2)
                g.memset(ones128[:, :], 1.0)

            @blk.sync
            def _(s):
                lds = ctrs["lds"]
                for nm, dst in [("ampb", ampb_t), ("phb", phb_t),
                                ("ln3g", ln3g_t), ("ln3b", ln3b_t),
                                ("ln4g", ln4g_t), ("ln4b", ln4b_t)]:
                    s.dma_start(out=dst[:, :], in_=hbm[nm][:, :]
                                ).then_inc(lds.sem, 16)
                    lds.inc(16)

            @blk.vector
            def _(v):
                v.wait_ge(ctrs["lds"].sem, ctrs["lds"].val)

        with contextlib.ExitStack() as ctx2:
            emit_l1ffn_precompute(nc, ctx2, sb, ctrs, hbm)
        if debug:
            with nc.Block() as blk:
                @blk.gpsimd
                def _(g):
                    ld = ctrs["ld"]
                    for a, b in [("dbg_Mg", "Mg"), ("dbg_Wcg", "Wcg"),
                                 ("dbg_bg", "bg"), ("dbg_bc", "bc_scr")]:
                        g.dma_start(out=hbm[a][:, :], in_=hbm[b][:, :]
                                    ).then_inc(ld.sem, 16)
                        ld.inc(16)
                    g.wait_ge(ld.sem, ld.val)

        with contextlib.ExitStack() as ctx3:
            E3 = ctx3.enter_context
            sb["U"] = E3(nc.sbuf_tensor("U", [128, NCT * L1F_TC], BF16))
            sb["A"] = E3(nc.sbuf_tensor("A", [128, NCT * L1F_TC], BF16))
            sb["tmp1"] = E3(nc.sbuf_tensor("tmp1", [128, 2 * L1F_TC], BF16))
            sb["tmp2"] = E3(nc.sbuf_tensor("tmp2", [128, 2 * L1F_TC], BF16))
            sb["sqr"] = E3(nc.sbuf_tensor("sqr", [128, L1F_TC], BF16))
            sb["mb"] = E3(nc.sbuf_tensor("mb", [128, L1F_TC], F32))
            sb["rb"] = E3(nc.sbuf_tensor("rb", [128, L1F_TC], F32))
            sb["wbuf"] = E3(nc.sbuf_tensor("wbuf", [128, 2 * 8192], BF16))
            sb["xrw"] = E3(nc.sbuf_tensor("xrw", [128, 8 * L1F_TC], F32))
            emit_l1ffn_sublayer(nc, sb, ctrs, hbm, hbm["xT"], hbm["yT"])
    return nc



NH = 16
HD = 128
SK = 8192


def emit_l0attn_sublayer(nc, ctx2, sb, ctrs, hbm, xin, xout, upto=99):
    ld, mm, sq, st, ev, out, wld = (ctrs["ld"], ctrs["mm"], ctrs["sq"],
                                    ctrs["st"], ctrs["ev"], ctrs["out"],
                                    ctrs["wld"])
    E = ctx2.enter_context
    tmpn = E(nc.sbuf_tensor("atmpn", [128, 2 * 2048], BF16))
    tmpr = E(nc.sbuf_tensor("atmpr", [128, 1024], BF16))
    kstg = E(nc.sbuf_tensor("akstg", [128, 2048], BF16))
    cosT = E(nc.sbuf_tensor("acosT", [128, NCT * 64], F32))
    sinT = E(nc.sbuf_tensor("asinT", [128, NCT * 64], F32))
    cosR = E(nc.sbuf_tensor("acosR", [128, NH * 64], BF16))
    sinR = E(nc.sbuf_tensor("asinR", [128, NH * 64], BF16))
    sqr2 = E(nc.sbuf_tensor("asqr2", [128, 2048], BF16))
    mb2 = E(nc.sbuf_tensor("amb2", [128, 2048], F32))
    rb2 = E(nc.sbuf_tensor("arb2", [128, 2048], F32))
    q2c = E(nc.sbuf_tensor("aq2c", [128, NH], F32))
    k2c = E(nc.sbuf_tensor("ak2c", [128, NH], F32))
    negm = E(nc.sbuf_tensor("anegm", [128, NH], F32))
    rown = E(nc.sbuf_tensor("arown", [1, 64], F32))
    posr = E(nc.sbuf_tensor("aposr", [1, 128], F32))
    frow = E(nc.sbuf_tensor("afrow", [1, 64], F32))
    ident = E(nc.sbuf_tensor("aident", [128, 128], BF16))

    with contextlib.ExitStack() as cU:
        U = cU.enter_context(nc.sbuf_tensor("aU", [128, NCT * T], BF16))
        ps = cU.enter_context(nc.psum_tensor("apsA", [128, 4096], F32))
        with nc.Block() as blk:
            ld0 = ld.val

            @blk.gpsimd
            def _(g):
                g.dma_start(
                    out=U[:, :].rearrange("p (ct t) -> p ct t", ct=NCT),
                    in_=xin[:, :].rearrange("(ct p) t -> p ct t", p=128),
                ).then_inc(ld.sem, 16)
                ld.inc(16)
                g.dma_start(out=posr[0:1, :], in_=hbm["posrow"][:, :]
                            ).then_inc(ld.sem, 16)
                ld.inc(16)
                g.dma_start(out=frow[0:1, :], in_=hbm["invfreq"][:, :]
                            ).then_inc(ld.sem, 16)
                ld.inc(16)
                g.iota(mb2[:, 0:128], [[0, 128]], channel_multiplier=1,
                       allow_small_or_imprecise_dtypes=True)
                g.iota(rb2[:, 0:128], [[1, 128]], channel_multiplier=0,
                       allow_small_or_imprecise_dtypes=True
                       ).then_inc(ev.sem, 1)
                ev.inc()
                g.wait_ge(ld.sem, ld0 + 48)

            @blk.vector
            def _(v):
                v.wait_ge(ld.sem, ld0 + 48)
                v.wait_ge(ev.sem, ev.val)
                v.tensor_tensor(ident[:, :], mb2[:, 0:128], rb2[:, 0:128],
                                ALU.is_equal)

        emit_ln(nc, ctrs, src=U, dst=U, sqr=sqr2, mb=mb2, rb=rb2, ps=ps,
                ones128=sb["ones128"], eps_ap=sb["eps_ap"], tlen=T)

        with nc.Block() as blk:
            @blk.gpsimd
            def _(g):
                g.dma_start(out=hbm["u_scr"][:, :], in_=U[:, :]
                            ).then_inc(ld.sem, 16)
                ld.inc(16)
                g.wait_ge(ld.sem, ld.val)

    with contextlib.ExitStack() as cT, nc.Block() as blk:
        ps = cT.enter_context(nc.psum_tensor("apsT", [128, 4096], F32))
        mm0, ev0, st0 = mm.val, ev.val, st.val

        @blk.tensor
        def _(t):
            ins = None
            for tt in range(NCT):
                ins = t.matmul(ps[:, tt * 64:(tt + 1) * 64], posr[0:1, :],
                               frow[0:1, :], start=True, stop=True)
            ins = t.matmul(ps[:, 2048:2048 + 64], sb["ones1f"][0:1, :],
                           frow[0:1, :], start=True, stop=True)
            ins.then_inc(mm.sem, 1)
            mm.inc()

        @blk.vector
        def _(v):
            v.wait_ge(mm.sem, mm0 + 1)
            v.tensor_copy(cosT[:, :], ps[:, 0:NCT * 64])
            v.tensor_copy(rb2[:, 0:64], ps[:, 2048:2048 + 64])
            for tt in range(1, NCT):
                v.scalar_tensor_tensor(
                    cosT[:, tt * 64:(tt + 1) * 64], rb2[:, 0:64],
                    float(tt * 128), cosT[:, tt * 64:(tt + 1) * 64],
                    ALU.mult, ALU.add)
            for k in range(10, -1, -1):
                stepv = float((2 ** k) * 2 * PI)
                v.tensor_scalar(sinT[:, :], cosT[:, :], stepv, None,
                                ALU.is_gt)
                v.scalar_tensor_tensor(cosT[:, :], sinT[:, :], -stepv,
                                       cosT[:, :], ALU.mult, ALU.add)
            v.tensor_scalar(sinT[:, :], cosT[:, :], PI, None, ALU.is_gt)
            v.scalar_tensor_tensor(cosT[:, :], sinT[:, :], -2 * PI,
                                   cosT[:, :], ALU.mult,
                                   ALU.add).then_inc(ev.sem, 1)
            ev.inc()

        @blk.scalar
        def _(s):
            s.wait_ge(ev.sem, ev0 + 1)
            s.activation(sinT[:, :], cosT[:, :], AF.Sin)
            s.activation(cosT[:, :], cosT[:, :], AF.Abs)
            s.activation(cosT[:, :], cosT[:, :], AF.Sin, scale=-1.0,
                         bias=sb["halfpi"][:, 0:1]).then_inc(st.sem, 1)
            st.inc()

    if upto <= 2:
        return
    QY = E(nc.sbuf_tensor("aQY", [128, NH * T], BF16))

    with contextlib.ExitStack() as cW:
        W = cW.enter_context(nc.sbuf_tensor("aW", [128, NCT * T], BF16))
        ustg = cW.enter_context(nc.sbuf_tensor("austg", [128, 2 * 2048],
                                               BF16))
        psF = cW.enter_context(nc.psum_tensor("apsF", [128, 2048], F32))
        pst = cW.enter_context(nc.psum_tensor("apst", [128, 2048], BF16))
        _l0attn_qkv(nc, sb, ctrs, hbm, W, ustg, QY, tmpn, tmpr, kstg,
                    cosT, sinT, cosR, sinR, sqr2, q2c, k2c, ident, pst,
                    psF, projs=sb.get("_projs", ("v", "k", "q")),
                    norms=sb.get("_norms", True))

    if upto <= 3:
        return
    with nc.Block() as blk:
        @blk.gpsimd
        def _(g):
            for k, (ni, no) in enumerate((("kT_own", "kT_g"),
                                          ("v_own", "v_g"),
                                          ("kn_own", "kn_g"))):
                g.collective_compute(
                    "AllGather", ALU.bypass, replica_groups=GROUPS,
                    ins=[hbm[ni][:, :]], outs=[hbm[no][:, :]],
                ).then_inc(sb["ccsems"][k].sem, 1)
            for k in range(3):
                g.wait_ge(sb["ccsems"][k].sem, 1)

    if upto <= 4:
        return
    with contextlib.ExitStack() as cS, nc.Block() as blk:
        ps = cS.enter_context(nc.psum_tensor("apsS", [128, 512], F32))
        ld0, mm0, ev0, st0 = ld.val, mm.val, ev.val, st.val

        @blk.gpsimd
        def _(g):
            g.dma_start(out=rown[0:1, :], in_=hbm["kn_g"][:, :]
                        .rearrange("r h -> () (r h)")).then_inc(ld.sem, 16)
            ld.inc(16)

        @blk.vector
        def _(v):
            v.wait_ge(ld.sem, ld0 + 16)
            v.tensor_tensor(rown[0:1, 0:16], rown[0:1, 0:16],
                            rown[0:1, 16:32], ALU.max)
            v.tensor_tensor(rown[0:1, 32:48], rown[0:1, 32:48],
                            rown[0:1, 48:64], ALU.max)
            v.tensor_tensor(rown[0:1, 0:16], rown[0:1, 0:16],
                            rown[0:1, 32:48], ALU.max).then_inc(ev.sem, 1)
            ev.inc()

        @blk.tensor
        def _(t):
            t.wait_ge(ev.sem, ev0 + 1)
            t.matmul(ps[:, 0:16], sb["ones1f"][0:1, :], rown[0:1, 0:16],
                     start=True, stop=True).then_inc(mm.sem, 1)
            mm.inc()

        @blk.vector
        def _(v):
            v.wait_ge(mm.sem, mm0 + 1)
            v.tensor_copy(k2c[:, :], ps[:, 0:16]).then_inc(ev.sem, 1)
            ev.inc()

        @blk.scalar
        def _(s):
            s.wait_ge(ev.sem, ev0 + 2)
            s.activation(negm[:, :], q2c[:, :], AF.Sqrt)
            s.activation(k2c[:, :], k2c[:, :], AF.Sqrt).then_inc(st.sem, 1)
            st.inc()

        @blk.vector
        def _(v):
            v.wait_ge(st.sem, st0 + 1)
            v.tensor_tensor(negm[:, :], negm[:, :], k2c[:, :], ALU.mult)
            v.tensor_scalar(negm[:, :], negm[:, :],
                            -1.02 / float(np.sqrt(128.0)), None, ALU.mult)

    if upto <= 5:
        return
    with contextlib.ExitStack() as cF:
        kvK = cF.enter_context(nc.sbuf_tensor("akvK", [128, SK], BF16))
        kvV = cF.enter_context(nc.sbuf_tensor("akvV", [128, SK], BF16))
        expb = cF.enter_context(nc.sbuf_tensor("aexpb", [128, 4 * 512],
                                               BF16))
        rz = cF.enter_context(nc.sbuf_tensor("arz", [128, 512], F32))
        psfl = cF.enter_context(nc.psum_tensor("apsfl", [128, 4096], F32))
        _l0attn_flash(nc, sb, ctrs, hbm, QY, kvK, kvV, expb, rz, negm,
                      psfl)

    if upto <= 6:
        return
    with contextlib.ExitStack() as cO:
        xr8 = cO.enter_context(nc.sbuf_tensor("axr8", [128, 4 * T], F32))
        wbufo = cO.enter_context(nc.sbuf_tensor("awbufo", [128, 2 * 8192],
                                                BF16))
        pso = cO.enter_context(nc.psum_tensor("apsO", [128, 4096], F32))
        woT_r = hbm["woT_scr"].rearrange("(ct p) o -> p ct o", p=128)
        for half in range(4):
            o0 = half * 4
            with nc.Block() as blk:
                @blk.scalar
                def _(s):
                    lda = ctrs["lda"]
                    for j in range(4):
                        s.dma_start(out=xr8[:, j * T:(j + 1) * T],
                                    in_=xin[(o0 + j) * 128:(o0 + j + 1) * 128, :]
                                    ).then_inc(lda.sem, 16)
                        lda.inc(16)
                    s.wait_ge(lda.sem, lda.val)

            def wo_epi_v(v, ot, psv):
                return v.scalar_tensor_tensor(
                    xr8[:, ot * T:(ot + 1) * T], psv, 1.0,
                    xr8[:, ot * T:(ot + 1) * T], ALU.mult, ALU.add)

            emit_stage16(nc, ctrs,
                         w_r=woT_r[:, :, o0 * 128:(o0 + 4) * 128],
                         rhs=QY, wbuf=wbufo,
                         out_nt=4, ps=pso, tlen=T, epi_vector=wo_epi_v)

            with nc.Block() as blk:
                @blk.gpsimd
                def _(g):
                    for j in range(4):
                        g.dma_start(
                            out=xout[(o0 + j) * 128:(o0 + j + 1) * 128, :],
                            in_=xr8[:, j * T:(j + 1) * T]
                            ).then_inc(out.sem, 16)
                        out.inc(16)
                    g.wait_ge(out.sem, out.val)


def _l0attn_qkv(nc, sb, ctrs, hbm, W, ustg, QY, tmpn, tmpr, kstg,
                cosT, sinT, cosR, sinR, sqr2, q2c, k2c, ident, pst, psF,
                projs=("v", "k", "q"), norms=True):
    ld, mm, st, ev, out, wld = (ctrs["ld"], ctrs["mm"], ctrs["st"],
                                ctrs["ev"], ctrs["out"], ctrs["wld"])
    out2 = ctrs["out2"]
    u_r = hbm["u_scr"].rearrange("p (ct t) -> p ct t", ct=NCT)

    for proj in projs:
        scr = hbm[{"v": "wvT_scr", "k": "wkT_scr", "q": "wqT_scr"}[proj]]
        per = 1 if proj == "v" else (1 + NH)
        with nc.Block() as blk:
            @blk.sync
            def _(s):
                for q4 in range(4):
                    s.dma_start(
                        out=W[:, q4 * 4 * T:(q4 + 1) * 4 * T]
                        .rearrange("p (ct o) -> p ct o", ct=4),
                        in_=scr.rearrange("(ct p) o -> p ct o", p=128)
                        [:, q4 * 4:(q4 + 1) * 4, :],
                    ).then_inc(wld.sem, 16)
                    wld.inc(16)
                    s.wait_ge(wld.sem, wld.val)

        with nc.Block() as blk:
            ld0, mm0, ev0, st0, out20 = (ld.val, mm.val, ev.val, st.val,
                                         out2.val)

            @blk.gpsimd
            def _(g):
                for tt in range(NCT):
                    buf = tt % 2
                    if tt >= 2:
                        g.wait_ge(mm.sem, mm0 + (tt - 2) * per + 1)
                    g.dma_start(
                        out=ustg[:, buf * 2048:(buf + 1) * 2048]
                        .rearrange("p (ct c) -> p ct c", ct=NCT),
                        in_=u_r[:, :, tt * 128:(tt + 1) * 128],
                    ).then_inc(ld.sem, 16)
                    ld.inc(16)
                    g.wait_ge(ld.sem, ld.val)

            @blk.tensor
            def _(t):
                for tt in range(NCT):
                    buf = tt % 2
                    t.wait_ge(ld.sem, ld0 + (tt + 1) * 16)
                    if tt >= 1:
                        t.wait_ge(ev.sem, ev0 + (tt - 1) * per + 1)
                    ins = None
                    for oc in range(4):
                        for ct in range(NCT):
                            ins = t.matmul(
                                psF[:, oc * 512:(oc + 1) * 512],
                                ustg[:, buf * 2048 + ct * 128:
                                     buf * 2048 + (ct + 1) * 128],
                                W[:, ct * T + oc * 512:
                                  ct * T + (oc + 1) * 512],
                                start=(ct == 0), stop=(ct == NCT - 1))
                    ins.then_inc(mm.sem, 1)
                    mm.inc()
                    if proj != "v":
                        t.wait_ge(ev.sem, ev0 + tt * per + 1)
                        for h in range(NH):
                            hp = h % 2
                            if tt > 0 or h >= 2:
                                t.wait_ge(ev.sem, ev0 + tt * per + h)
                            ins = t.transpose(
                                pst[:, hp * 1024:hp * 1024 + 128],
                                tmpn[:, 2048 + h * 128:
                                     2048 + (h + 1) * 128],
                                ident[:, :])
                            ins.then_inc(mm.sem, 1)
                            mm.inc()

            @blk.vector
            def _(v):
                for tt in range(NCT):
                    v.wait_ge(mm.sem, mm0 + tt * per + 1)
                    if proj == "v":
                        if tt > 0:
                            v.wait_ge(out2.sem, out20 + tt * 16)
                        ins = v.tensor_copy(tmpn[:, 0:2048], psF[:, 0:2048])
                        ins.then_inc(ev.sem, 1)
                        ev.inc()
                        continue
                    v.tensor_copy(tmpn[:, 0:2048], psF[:, 0:2048])
                    for h in range(NH):
                        v.tensor_copy(cosR[:, h * 64:(h + 1) * 64],
                                      cosT[:, tt * 64:(tt + 1) * 64])
                        v.tensor_copy(sinR[:, h * 64:(h + 1) * 64],
                                      sinT[:, tt * 64:(tt + 1) * 64])
                    x = tmpn[:, 0:2048].rearrange(
                        "p (h two f) -> p h two f", h=NH, two=2)
                    y = tmpn[:, 2048:4096].rearrange(
                        "p (h two f) -> p h two f", h=NH, two=2)
                    cR = cosR[:, :].rearrange("p (h f) -> p h f", h=NH)
                    sR = sinR[:, :].rearrange("p (h f) -> p h f", h=NH)
                    tr = tmpr[:, :].rearrange("p (h f) -> p h f", h=NH)
                    if proj == "k" and tt > 0:
                        v.wait_ge(out2.sem, out20 + tt * 16)
                    v.tensor_tensor(y[:, :, 0, :], x[:, :, 0, :], cR,
                                    ALU.mult)
                    v.tensor_tensor(tr, x[:, :, 1, :], sR, ALU.mult)
                    v.tensor_tensor(y[:, :, 0, :], y[:, :, 0, :], tr,
                                    ALU.subtract)
                    v.tensor_tensor(y[:, :, 1, :], x[:, :, 0, :], sR,
                                    ALU.mult)
                    v.tensor_tensor(tr, x[:, :, 1, :], cR, ALU.mult)
                    ins = v.tensor_tensor(y[:, :, 1, :], y[:, :, 1, :], tr,
                                          ALU.add)
                    ins.then_inc(ev.sem, 1)
                    ev.inc()
                    for h in range(NH):
                        v.wait_ge(mm.sem, mm0 + tt * per + 1 + h + 1)
                        hp = h % 2
                        if proj == "q":
                            dst = QY[:, h * T + tt * 128:
                                     h * T + (tt + 1) * 128]
                        else:
                            dst = kstg[:, h * 128:(h + 1) * 128]
                        ins = v.tensor_copy(
                            dst, pst[:, hp * 1024:hp * 1024 + 128])
                        ins.then_inc(ev.sem, 1)
                        ev.inc()

            if proj == "v":
                @blk.scalar
                def _(s):
                    for tt in range(NCT):
                        s.wait_ge(ev.sem, ev0 + tt + 1)
                        s.dma_start(
                            out=hbm["v_own"][tt * 128:(tt + 1) * 128, :],
                            in_=tmpn[:, 0:2048]).then_inc(out2.sem, 16)
                        out2.inc(16)
                        s.wait_ge(out2.sem, out2.val)
            elif proj == "k":
                @blk.scalar
                def _(s):
                    for tt in range(NCT):
                        s.wait_ge(ev.sem, ev0 + (tt + 1) * per)
                        s.dma_start(
                            out=hbm["kT_own"]
                            .rearrange("(h d) t -> d h t", h=NH)
                            [:, :, tt * 128:(tt + 1) * 128],
                            in_=kstg[:, :].rearrange("p (h t) -> p h t",
                                                     h=NH),
                        ).then_inc(out2.sem, 16)
                        out2.inc(16)
                        s.wait_ge(out2.sem, out2.val)

        if not norms:
            continue
        if proj == "k":
            _l0attn_norm_pass(nc, sb, ctrs, k2c, sqr2, tmpn, psF,
                              src_hbm=hbm["kT_own"])
            with nc.Block() as blk:
                @blk.gpsimd
                def _(g):
                    g.dma_start(out=hbm["kn_own"][:, :],
                                in_=k2c[0:1, 0:NH]).then_inc(ld.sem, 16)
                    ld.inc(16)
                    g.wait_ge(ld.sem, ld.val)
        elif proj == "q":
            _l0attn_norm_pass(nc, sb, ctrs, q2c, sqr2, tmpn, psF,
                              src_sbuf=QY)


def _l0attn_norm_pass(nc, sb, ctrs, nacc, sqr2, tmpn, psF,
                      src_hbm=None, src_sbuf=None):
    ld, mm, ev = ctrs["ld"], ctrs["mm"], ctrs["ev"]
    with nc.Block() as blk:
        ld0, mm0, ev0 = ld.val, mm.val, ev.val

        if src_hbm is not None:
            @blk.gpsimd
            def _(g):
                for h in range(NH):
                    if h >= 1:
                        g.wait_ge(ev.sem, ev0 + 2 * h - 1)
                    g.dma_start(out=tmpn[:, 0:2048],
                                in_=src_hbm[h * 128:(h + 1) * 128, :]
                                ).then_inc(ld.sem, 16)
                    ld.inc(16)
                    g.wait_ge(ld.sem, ld.val)

        @blk.vector
        def _(v):
            for h in range(NH):
                if src_hbm is not None:
                    v.wait_ge(ld.sem, ld0 + (h + 1) * 16)
                    srcv = tmpn[:, 0:2048]
                else:
                    srcv = src_sbuf[:, h * T:(h + 1) * T]
                if h >= 1:
                    v.wait_ge(mm.sem, mm0 + h)
                v.tensor_tensor(sqr2[:, :], srcv, srcv,
                                ALU.mult).then_inc(ev.sem, 1)
                ev.inc()
                v.wait_ge(mm.sem, mm0 + h + 1)
                ins = v.tensor_reduce(nacc[:, h:h + 1], psF[:, 0:2048],
                                      mybir.AxisListType.X, ALU.max)
                ins.then_inc(ev.sem, 1)
                ev.inc()

        @blk.tensor
        def _(t):
            for h in range(NH):
                t.wait_ge(ev.sem, ev0 + 2 * h + 1)
                ins = None
                for j in range(4):
                    ins = t.matmul(psF[:, j * 512:(j + 1) * 512],
                                   sb["ones128"][:, :],
                                   sqr2[:, j * 512:(j + 1) * 512],
                                   start=True, stop=True)
                ins.then_inc(mm.sem, 1)
                mm.inc()


def _l0attn_flash(nc, sb, ctrs, hbm, QY, kvK, kvV, expb, rz, negm, ps):
    ld, mm, st, ev = ctrs["ld"], ctrs["mm"], ctrs["st"], ctrs["ev"]
    kT_g = hbm["kT_g"].rearrange("(r h d) t -> r h d t", r=4, h=NH)
    v_g = hbm["v_g"].rearrange("(r t) c -> r t c", r=4)
    NST = SK // 128
    with nc.Block() as blk:
        ld0, mm0, st0, ev0 = ld.val, mm.val, st.val, ev.val

        def mm_before(i):
            return mm0 + i * 2 * NST

        def st_before(i):
            return st0 + i * NST

        @blk.gpsimd
        def _(g):
            for h in range(NH):
                if h >= 1:
                    g.wait_ge(ev.sem, ev0 + h * 4)
                for r in range(4):
                    g.dma_start(out=kvK[:, r * 2048:(r + 1) * 2048],
                                in_=kT_g[r, h, :, :]).then_inc(ld.sem, 16)
                    ld.inc(16)
                    g.wait_ge(ld.sem, ld.val)
                for r in range(4):
                    g.dma_start(
                        out=kvV[:, r * 2048:(r + 1) * 2048]
                        .rearrange("p (s d) -> p s d", s=16),
                        in_=v_g[r].rearrange("(s p) c -> p s c", p=128)
                        [:, :, h * 128:(h + 1) * 128],
                    ).then_inc(ld.sem, 16)
                    ld.inc(16)
                    g.wait_ge(ld.sem, ld.val)

        @blk.tensor
        def _(t):
            for h in range(NH):
                t.wait_ge(ld.sem, ld0 + (h + 1) * 128)
                for qc in range(4):
                    i = h * 4 + qc
                    zb = 4 + (i % 2) * 2
                    ob = 5 + (i % 2) * 2
                    if i >= 2:
                        t.wait_ge(ev.sem, ev0 + i - 1)
                    qs = QY[:, h * T + qc * 512:h * T + (qc + 1) * 512]
                    for stt in range(NST):
                        sbk = stt % 2
                        if i > 0 or stt >= 2:
                            t.wait_ge(st.sem, st_before(i) + stt - 1)
                        t.matmul(ps[:, sbk * 512:(sbk + 1) * 512],
                                 kvK[:, stt * 128:(stt + 1) * 128], qs,
                                 start=True, stop=True).then_inc(mm.sem, 1)
                        mm.inc()
                        t.wait_ge(st.sem, st_before(i) + stt + 1)
                        eb = expb[:, (stt % 4) * 512:(stt % 4 + 1) * 512]
                        t.matmul(ps[:, zb * 512:(zb + 1) * 512],
                                 sb["ones128"][:, :], eb,
                                 start=(stt == 0), stop=(stt == NST - 1))
                        ins = t.matmul(ps[:, ob * 512:(ob + 1) * 512],
                                       kvV[:, stt * 128:(stt + 1) * 128],
                                       eb, start=(stt == 0),
                                       stop=(stt == NST - 1))
                        ins.then_inc(mm.sem, 1)
                        mm.inc()

        @blk.scalar
        def _(s):
            for h in range(NH):
                for qc in range(4):
                    i = h * 4 + qc
                    for stt in range(NST):
                        sbk = stt % 2
                        s.wait_ge(mm.sem, mm_before(i) + 2 * stt + 1)
                        if stt >= 4:
                            s.wait_ge(mm.sem, mm_before(i) + 2 * (stt - 3))
                        s.activation(
                            expb[:, (stt % 4) * 512:(stt % 4 + 1) * 512],
                            ps[:, sbk * 512:(sbk + 1) * 512], AF.Exp,
                            scale=float(1.0 / np.sqrt(128.0)),
                            bias=negm[:, h:h + 1]).then_inc(st.sem, 1)
                        st.inc()

        @blk.vector
        def _(v):
            for h in range(NH):
                for qc in range(4):
                    i = h * 4 + qc
                    zb = 4 + (i % 2) * 2
                    ob = 5 + (i % 2) * 2
                    v.wait_ge(mm.sem, mm_before(i) + 2 * NST)
                    v.reciprocal(rz[:, :], ps[:, zb * 512:(zb + 1) * 512])
                    ins = v.tensor_tensor(
                        QY[:, h * T + qc * 512:h * T + (qc + 1) * 512],
                        ps[:, ob * 512:(ob + 1) * 512], rz[:, :],
                        ALU.mult)
                    ins.then_inc(ev.sem, 1)
                    ev.inc()


def _pcol2(v):
    v = np.asarray(v)
    n = v.shape[1] // 128
    return np.ascontiguousarray(
        np.concatenate([v[l].reshape(n, 128).T for l in range(2)], axis=1))


def l1ffn_in_maps(inp, shards_xT):
    amp_w = np.ascontiguousarray(np.asarray(inp["L1_ffn_amp_w"]))
    ph_w = np.ascontiguousarray(np.asarray(inp["L1_ffn_ph_w"]))
    ent_w = np.asarray(inp["L1_ffn_ent_w"])
    ent_wT = np.ascontiguousarray(ent_w.transpose(0, 2, 1))
    ent_pw = np.ascontiguousarray(np.asarray(inp["L1_ffn_ent_pw"]))
    ent_b = np.asarray(inp["L1_ffn_ent_b"])
    ent_bcol = np.ascontiguousarray(
        np.concatenate([ent_b[l].reshape(64, 128).T for l in range(2)],
                       axis=1))
    ent_pb = np.asarray(inp["L1_ffn_ent_pb"])
    path_w = np.asarray(inp["L1_ffn_path_w"])
    maps = []
    for c in range(NCORES):
        r0, r1 = c * 256, (c + 1) * 256
        maps.append({
            "xT": shards_xT[c],
            "amp_w": amp_w, "ph_w": ph_w,
            "ent_wT_sl": np.ascontiguousarray(ent_wT[:, :, r0:r1]),
            "ent_pw": ent_pw,
            "ent_pw_osl": np.ascontiguousarray(ent_pw[:, :, r0:r1]),
            "path_w_sl": np.ascontiguousarray(path_w[:, :, r0:r1, :]),
            "phase": np.ascontiguousarray(np.asarray(inp["L1_ffn_phase"])),
            "path_b": np.ascontiguousarray(np.asarray(inp["L1_ffn_path_b"])),
            "ent_pb_sl": np.ascontiguousarray(ent_pb[:, r0:r1]),
            "ent_bcol": ent_bcol,
            "ampb": _pcol(inp["L1_ffn_amp_b"], 16),
            "phb": _pcol(inp["L1_ffn_ph_b"], 16),
            "ln3g": _pcol2(inp["L1_ffn_ln3_g"]),
            "ln3b": _pcol2(inp["L1_ffn_ln3_b"]),
            "ln4g": _pcol2(inp["L1_ffn_ln4_g"]),
            "ln4b": _pcol2(inp["L1_ffn_ln4_b"]),
        })
    return maps


def run_l1ffn(x, inp):
    nc = build_l1ffn_launch()
    shards = _shard_xT(x)
    maps = l1ffn_in_maps(inp, shards)
    res = run_bass_kernel_spmd(nc, maps, core_ids=list(range(NCORES)))
    return _unshard_xT([res.results[c]["yT"] for c in range(NCORES)])


def kernel(**inputs):
    inp = {k: np.asarray(v) for k, v in inputs.items()}
    x = inp["L0_attn_x"].astype(np.float32)
    x = run_l0ffn(x, inp["L0_ffn_w_gate"], inp["L0_ffn_w_up"],
                  inp["L0_ffn_w_down"], inp["L0_ffn_s_gate"],
                  inp["L0_ffn_s_up"], inp["L0_ffn_s_down"])
    x = run_l0attn(x, inp)
    x = run_l0ffn(x, inp["L0_ffn_w_gate"], inp["L0_ffn_w_up"],
                  inp["L0_ffn_w_down"], inp["L0_ffn_s_gate"],
                  inp["L0_ffn_s_up"], inp["L0_ffn_s_down"])
    x = run_l1ffn(x, inp)
    x = run_l1attn(x, inp)
    x = run_l1ffn(x, inp)
    return np.asarray(x, dtype=np.float32).reshape(2, 8192, 2048)



L1A_NCH = 16


def emit_l1attn_sublayer(nc, ctx2, sb, ctrs, hbm, xin, xout):
    ld, lds, lda, mm, st, ev, out, out2, wld = (
        ctrs["ld"], ctrs["lds"], ctrs["lda"], ctrs["mm"], ctrs["st"],
        ctrs["ev"], ctrs["out"], ctrs["out2"], ctrs["wld"])
    E = ctx2.enter_context
    mcols = {}
    for nm in ("maa_x", "maa_r", "maa_k", "maa_v", "maa_w", "q_b", "k_b",
               "v_b", "td", "ln_g", "ln_b"):
        mcols[nm] = E(nc.sbuf_tensor("c_" + nm, [128, 16], F32))
    bcol = E(nc.sbuf_tensor("l1bcol", [128, 16], F32))
    umask = E(nc.sbuf_tensor("l1umask", [128, 3], F32))
    twT = E(nc.sbuf_tensor("l1twT", [64, 2048], BF16))
    tdw2 = E(nc.sbuf_tensor("l1tdw2", [64, 2048], BF16))
    trimask = E(nc.sbuf_tensor("l1tri", [128, 64], BF16))
    ident = E(nc.sbuf_tensor("l1ident", [128, 128], BF16))

    with nc.Block() as blk:
        ld0, ev0 = ld.val, ev.val

        @blk.gpsimd
        def _(g):
            wc = ctrs["wconv"]
            for a, b in (("q_w", "qw_scr"), ("k_w", "kw_scr"),
                         ("v_w", "vw_scr"), ("o_w", "ow_scr")):
                g.dma_start(out=hbm[b][:, :], in_=hbm[a][:, :]
                            ).then_inc(wc.sem, 16)
                wc.inc(16)
            for nm in mcols:
                g.dma_start(out=mcols[nm][:, :], in_=hbm["col_" + nm][:, :]
                            ).then_inc(ld.sem, 16)
                ld.inc(16)
            g.dma_start(out=umask[:, :], in_=hbm["umask"][:, :]
                        ).then_inc(ld.sem, 16)
            ld.inc(16)
            g.dma_start(out=tdw2[:, :], in_=hbm["td_w2"][:, :]
                        ).then_inc(ld.sem, 16)
            ld.inc(16)
            g.iota(sb["mb"][:, 0:128], [[0, 128]], channel_multiplier=1,
                   allow_small_or_imprecise_dtypes=True)
            g.iota(sb["rb"][:, 0:128], [[1, 128]], channel_multiplier=0,
                   allow_small_or_imprecise_dtypes=True
                   ).then_inc(ev.sem, 1)
            ev.inc()
            g.wait_ge(wc.sem, wc.val)
            g.wait_ge(ld.sem, ld.val)

        @blk.vector
        def _(v):
            v.wait_ge(ev.sem, ev0 + 1)
            v.tensor_tensor(ident[:, :], sb["mb"][:, 0:128],
                            sb["rb"][:, 0:128], ALU.is_equal)
            v.tensor_tensor(trimask[0:64, :], sb["mb"][0:64, 0:64],
                            sb["rb"][0:64, 0:64], ALU.is_le)
            v.tensor_scalar(sb["mb"][64:128, 0:64], sb["mb"][64:128, 0:64],
                            -64.0, None, ALU.add)
            v.tensor_tensor(trimask[64:128, :], sb["mb"][64:128, 0:64],
                            sb["rb"][64:128, 0:64], ALU.is_le)

    with contextlib.ExitStack() as c0:
        U = c0.enter_context(nc.sbuf_tensor("l1U", [128, NCT * T], BF16))
        ps = c0.enter_context(nc.psum_tensor("l1psA", [128, 4096], F32))
        with nc.Block() as blk:
            ld0 = ld.val

            @blk.gpsimd
            def _(g):
                g.dma_start(
                    out=U[:, :].rearrange("p (ct t) -> p ct t", ct=NCT),
                    in_=xin[:, :].rearrange("(ct p) t -> p ct t", p=128),
                ).then_inc(ld.sem, 16)
                ld.inc(16)
                g.wait_ge(ld.sem, ld0 + 16)

        emit_ln(nc, ctrs, src=U, dst=U, sqr=sb["sqr2"], mb=sb["mb"],
                rb=sb["rb"], ps=ps, ones128=sb["ones128"],
                eps_ap=sb["eps_ap"], tlen=T)

        with nc.Block() as blk:
            ld0 = ld.val

            @blk.gpsimd
            def _(g):
                g.dma_start(out=hbm["u_scr"][:, :], in_=U[:, :]
                            ).then_inc(ld.sem, 16)
                ld.inc(16)
                for ct in range(NCT):
                    g.dma_start(
                        out=hbm["bnd_own"][0:1, ct * 128:(ct + 1) * 128]
                        .rearrange("q (p c) -> (q p) c", c=1),
                        in_=U[:, ct * T + T - 1:ct * T + T],
                    ).then_inc(ld.sem, 16)
                    ld.inc(16)
                g.wait_ge(ld.sem, ld.val)
                g.collective_compute(
                    "AllGather", ALU.bypass, replica_groups=GROUPS,
                    ins=[hbm["bnd_own"][:, :]], outs=[hbm["bnd_g"][:, :]],
                ).then_inc(sb["ccsems"][0].sem, 1)
                g.wait_ge(sb["ccsems"][0].sem, 1)

    with contextlib.ExitStack() as c0b, nc.Block() as blk:
        g4 = c0b.enter_context(nc.sbuf_tensor("l1g4", [4, 2048], F32))
        sel = c0b.enter_context(nc.sbuf_tensor("l1sel", [4, 1], F32))
        brow = c0b.enter_context(nc.sbuf_tensor("l1brow", [1, 2048], F32))
        psb = c0b.enter_context(nc.psum_tensor("l1psb", [128, 2048], F32))
        ld0, mm0, ev0 = ld.val, mm.val, ev.val

        @blk.gpsimd
        def _(g):
            g.dma_start(out=g4[:, :], in_=hbm["bnd_g"][:, :]
                        ).then_inc(ld.sem, 16)
            ld.inc(16)
            g.dma_start(out=sel[:, :], in_=hbm["prevsel"][:, :]
                        ).then_inc(ld.sem, 16)
            ld.inc(16)

        @blk.tensor
        def _(t):
            t.wait_ge(ld.sem, ld0 + 32)
            ins = None
            for j in range(4):
                ins = t.matmul(psb[0:1, j * 512:(j + 1) * 512], sel[:, :],
                               g4[:, j * 512:(j + 1) * 512],
                               start=True, stop=True)
            ins.then_inc(mm.sem, 1)
            mm.inc()

        @blk.vector
        def _(v):
            v.wait_ge(mm.sem, mm0 + 1)
            v.tensor_copy(brow[0:1, :],
                          psb[0:1, 0:2048]).then_inc(ev.sem, 1)
            ev.inc()

        @blk.gpsimd
        def _(g):
            g.wait_ge(ev.sem, ev0 + 1)
            g.dma_start(out=hbm["bprev_scr"][:, :], in_=brow[0:1, :]
                        ).then_inc(ld.sem, 16)
            ld.inc(16)
            for ct in range(NCT):
                g.dma_start(
                    out=bcol[:, ct:ct + 1],
                    in_=hbm["bprev_scr"][0:1, ct * 128:(ct + 1) * 128]
                    .rearrange("q (p c) -> (q p) c", c=1),
                ).then_inc(ld.sem, 16)
                ld.inc(16)
            g.wait_ge(ld.sem, ld.val)

    with contextlib.ExitStack() as c1:
        XV = c1.enter_context(nc.sbuf_tensor("l1XV", [128, NCT * T], BF16))
        mmT = c1.enter_context(nc.sbuf_tensor("l1mmT", [128, 2048], BF16))
        mmr = c1.enter_context(nc.sbuf_tensor("l1mmr", [32, 3 * 2048],
                                              BF16))
        w1b = c1.enter_context(nc.sbuf_tensor("l1w1b", [128, 16 * 128],
                                              BF16))
        w2b = c1.enter_context(nc.sbuf_tensor("l1w2b", [32, 4 * 2048],
                                              BF16))
        tdw1 = c1.enter_context(nc.sbuf_tensor("l1tdw1", [128, 16 * 64],
                                               BF16))
        ut = c1.enter_context(nc.sbuf_tensor("l1ut", [128, 2 * 2048], BF16))
        dxt = c1.enter_context(nc.sbuf_tensor("l1dxt", [128, 2048], BF16))
        mtmp = c1.enter_context(nc.sbuf_tensor("l1mtmp", [128, 2048], F32))
        stg = c1.enter_context(nc.sbuf_tensor("l1stg", [128, 2 * 2048],
                                              BF16))
        wbuf = c1.enter_context(nc.sbuf_tensor("l1wbuf", [128, 2 * 8192],
                                               BF16))
        psP = c1.enter_context(nc.psum_tensor("l1psP", [128, 4096], F32))
        u_r = hbm["u_scr"].rearrange("p (ct t) -> p ct t", ct=NCT)

        with nc.Block() as blk:
            @blk.gpsimd
            def _(g):
                g.dma_start(out=w1b[:, :]
                            .rearrange("p (ct o) -> p ct o", ct=16),
                            in_=hbm["maa_w1"]
                            .rearrange("(ct p) o -> p ct o", p=128)
                            ).then_inc(ld.sem, 16)
                ld.inc(16)
                g.dma_start(out=w2b[:, :].rearrange("p (r o) -> p r o",
                                                    r=4),
                            in_=hbm["maa_w2"].rearrange("r d o -> d r o")
                            ).then_inc(ld.sem, 16)
                ld.inc(16)
                g.dma_start(out=tdw1[:, :]
                            .rearrange("p (ct o) -> p ct o", ct=16),
                            in_=hbm["td_w1"]
                            .rearrange("(ct p) o -> p ct o", p=128)
                            ).then_inc(ld.sem, 16)
                ld.inc(16)
                g.wait_ge(ld.sem, ld.val)

        def build_variant(vecname, use_m, mrow):
            with nc.Block() as blk:
                ld0, mm0, ev0 = ld.val, mm.val, ev.val

                @blk.gpsimd
                def _(g):
                    for ct in range(NCT):
                        buf = ct % 2
                        if ct >= 2:
                            g.wait_ge(ev.sem, ev0 + (ct - 1))
                        g.dma_start(out=ut[:, buf * 2048:(buf + 1) * 2048],
                                    in_=u_r[:, ct, :]).then_inc(ld.sem, 16)
                        ld.inc(16)
                        g.wait_ge(ld.sem, ld.val)

                if use_m:
                    @blk.tensor
                    def _(t):
                        for ct in range(NCT):
                            pp = ct % 2
                            if ct >= 2:
                                t.wait_ge(ev.sem, ev0 + ct - 1)
                            ins = None
                            for fc in range(4):
                                if mrow == 0:
                                    rr = mmT[0:32, fc * 512:(fc + 1) * 512]
                                else:
                                    rr = mmr[:, (mrow - 1) * 2048 + fc * 512:
                                             (mrow - 1) * 2048 + (fc + 1) * 512]
                                ins = t.matmul(
                                    psP[:, pp * 2048 + fc * 512:
                                        pp * 2048 + (fc + 1) * 512],
                                    w2b[0:32, mrow * 2048 + ct * 128:
                                        mrow * 2048 + (ct + 1) * 128],
                                    rr, start=True, stop=True)
                            ins.then_inc(mm.sem, 1)
                            mm.inc()

                @blk.vector
                def _(v):
                    for ct in range(NCT):
                        pp = ct % 2
                        buf = ct % 2
                        v.wait_ge(ld.sem, ld0 + (ct + 1) * 16)
                        uu = ut[:, buf * 2048:(buf + 1) * 2048]
                        v.tensor_tensor(dxt[:, 1:2048], uu[:, 0:2047],
                                        uu[:, 1:2048], ALU.subtract)
                        v.tensor_scalar(dxt[:, 0:1], uu[:, 0:1], -1.0,
                                        bcol[:, ct:ct + 1], ALU.mult,
                                        ALU.add)
                        if use_m:
                            v.wait_ge(mm.sem, mm0 + ct + 1)
                            v.tensor_scalar(
                                mtmp[:, :],
                                psP[:, pp * 2048:(pp + 1) * 2048],
                                1.0, mcols[vecname][:, ct:ct + 1],
                                ALU.mult, ALU.add)
                            v.tensor_tensor(dxt[:, :], dxt[:, :],
                                            mtmp[:, :], ALU.mult)
                        else:
                            v.tensor_scalar(dxt[:, :], dxt[:, :],
                                            mcols[vecname][:, ct:ct + 1],
                                            None, ALU.mult)
                        ins = v.tensor_tensor(XV[:, ct * T:(ct + 1) * T],
                                              uu, dxt[:, :], ALU.add)
                        ins.then_inc(ev.sem, 1)
                        ev.inc()

        build_variant("maa_x", False, None)
        with nc.Block() as blk:
            mm0, st0 = mm.val, st.val

            @blk.tensor
            def _(t):
                ins = None
                for fc in range(4):
                    for ct in range(NCT):
                        ins = t.matmul(
                            psP[:, fc * 512:(fc + 1) * 512],
                            w1b[:, ct * 128:(ct + 1) * 128],
                            XV[:, ct * T + fc * 512:ct * T + (fc + 1) * 512],
                            start=(ct == 0), stop=(ct == NCT - 1))
                ins.then_inc(mm.sem, 1)
                mm.inc()

            @blk.scalar
            def _(s):
                s.wait_ge(mm.sem, mm0 + 1)
                s.activation(mmT[:, :], psP[:, 0:2048],
                             AF.Tanh).then_inc(st.sem, 1)
                st.inc()

            @blk.gpsimd
            def _(g):
                g.wait_ge(st.sem, st0 + 1)
                for r in range(1, 4):
                    g.dma_start(out=mmr[:, (r - 1) * 2048:r * 2048],
                                in_=mmT[r * 32:(r + 1) * 32, :]
                                ).then_inc(ld.sem, 16)
                    ld.inc(16)
                    g.wait_ge(ld.sem, ld.val)

        build_variant("maa_w", True, 3)
        with nc.Block() as blk:
            mm0, st0 = mm.val, st.val

            @blk.tensor
            def _(t):
                ins = None
                for fc in range(4):
                    for ct in range(NCT):
                        ins = t.matmul(
                            psP[0:64, fc * 512:(fc + 1) * 512],
                            tdw1[:, ct * 64:(ct + 1) * 64],
                            XV[:, ct * T + fc * 512:ct * T + (fc + 1) * 512],
                            start=(ct == 0), stop=(ct == NCT - 1))
                ins.then_inc(mm.sem, 1)
                mm.inc()

            @blk.scalar
            def _(s):
                s.wait_ge(mm.sem, mm0 + 1)
                s.activation(twT[:, :], psP[0:64, 0:2048],
                             AF.Tanh).then_inc(st.sem, 1)
                st.inc()

        for vecname, mrow, wscr, bcolname, dst in (
                ("maa_r", 0, "qw_scr", "q_b", "qT_scr"),
                ("maa_k", 1, "kw_scr", "k_b", "kT_scr"),
                ("maa_v", 2, "vw_scr", "v_b", "vT_scr")):
            build_variant(vecname, True, mrow)
            w_r = hbm[wscr].rearrange("(ct p) o -> p ct o", p=128)

            def proj_epi_v(v, ot, psv, bname=bcolname):
                return v.tensor_scalar(stg[:, (ot % 2) * 2048:
                                           (ot % 2 + 1) * 2048],
                                       psv, 1.0,
                                       mcols[bname][:, ot:ot + 1],
                                       ALU.mult, ALU.add)

            def proj_exp(s, ot, dname=dst):
                s.dma_start(out=hbm[dname][:, ot * 2048:(ot + 1) * 2048],
                            in_=stg[:, (ot % 2) * 2048:(ot % 2 + 1) * 2048]
                            ).then_inc(out2.sem, 16)
                out2.inc(16)
                s.wait_ge(out2.sem, out2.val)

            emit_stage16(nc, ctrs, w_r=w_r, rhs=XV, wbuf=wbuf, out_nt=NCT,
                         ps=psP, tlen=T, epi_vector=proj_epi_v,
                         export_scalar=proj_exp)

    with contextlib.ExitStack() as cD:
        OT = cD.enter_context(nc.sbuf_tensor("l1OT", [128, NCT * T], BF16))
        _l1attn_scan(nc, cD, sb, ctrs, hbm, OT, twT, tdw2, mcols,
                     trimask, ident)

        _l1attn_cross(nc, cD, sb, ctrs, hbm, OT, umask)

        with contextlib.ExitStack() as cLN:
            psE = cLN.enter_context(nc.psum_tensor("l1psE", [128, 4096],
                                                   F32))
            emit_ln(nc, ctrs, src=OT, dst=OT, sqr=sb["sqr2"], mb=sb["mb"],
                    rb=sb["rb"], ps=psE, ones128=sb["ones128"],
                    eps_ap=sb["eps_ap"], tlen=T, gain=mcols["ln_g"],
                    bias=mcols["ln_b"])
        with contextlib.ExitStack() as cE:
            xr4 = cE.enter_context(nc.sbuf_tensor("l1xr4", [128, 4 * T],
                                                  F32))
            wbufo = cE.enter_context(nc.sbuf_tensor("l1wbufo",
                                                    [128, 2 * 8192], BF16))
            pso = cE.enter_context(nc.psum_tensor("l1psO", [128, 4096],
                                                  F32))
            ow_r = hbm["ow_scr"].rearrange("(ct p) o -> p ct o", p=128)
            for quarter in range(4):
                o0 = quarter * 4
                with nc.Block() as blk:
                    @blk.scalar
                    def _(s):
                        for j in range(4):
                            s.dma_start(
                                out=xr4[:, j * T:(j + 1) * T],
                                in_=xin[(o0 + j) * 128:(o0 + j + 1) * 128, :]
                                ).then_inc(lda.sem, 16)
                            lda.inc(16)
                        s.wait_ge(lda.sem, lda.val)

                def ow_epi_v(v, ot, psv):
                    return v.scalar_tensor_tensor(
                        xr4[:, ot * T:(ot + 1) * T], psv, 1.0,
                        xr4[:, ot * T:(ot + 1) * T], ALU.mult, ALU.add)

                emit_stage16(nc, ctrs,
                             w_r=ow_r[:, :, o0 * 128:(o0 + 4) * 128],
                             rhs=OT, wbuf=wbufo, out_nt=4, ps=pso, tlen=T,
                             epi_vector=ow_epi_v)

                with nc.Block() as blk:
                    @blk.gpsimd
                    def _(g):
                        for j in range(4):
                            g.dma_start(
                                out=xout[(o0 + j) * 128:(o0 + j + 1) * 128, :],
                                in_=xr4[:, j * T:(j + 1) * T]
                                ).then_inc(out.sem, 16)
                            out.inc(16)
                        g.wait_ge(out.sem, out.val)


def _l1attn_scan(nc, cD, sb, ctrs, hbm, OT, twT, tdw2, mcols, trimask,
                 ident):
    ld, mm, st, ev = ctrs["ld"], ctrs["mm"], ctrs["st"], ctrs["ev"]
    Sfull = cD.enter_context(nc.sbuf_tensor("l1Sf", [128, 1024], F32))
    Gown = cD.enter_context(nc.sbuf_tensor("l1Gown", [128, 16], F32))
    sb["_l1Sf"] = Sfull
    sb["_l1Gown"] = Gown
    cS = contextlib.ExitStack()
    E = cS.enter_context
    qt = E(nc.sbuf_tensor("l1qt", [128, 2048], BF16))
    kt = E(nc.sbuf_tensor("l1kt", [128, 2048], BF16))
    vt = E(nc.sbuf_tensor("l1vt", [128, 2048], BF16))
    eT = E(nc.sbuf_tensor("l1eT", [128, 2048], F32))
    cumT = E(nc.sbuf_tensor("l1cumT", [128, 2048], F32))
    ksT = E(nc.sbuf_tensor("l1ksT", [128, 2048], F32))
    zT = E(nc.sbuf_tensor("l1zT", [128, 2048], BF16))
    Sb = E(nc.sbuf_tensor("l1Sb", [128, 64], BF16))
    colt = E(nc.sbuf_tensor("l1colt", [128, 8], F32))
    fac = E(nc.sbuf_tensor("l1fac", [128, 128], BF16))
    kfE = E(nc.sbuf_tensor("l1kfE", [128, 128], BF16))
    kf1 = E(nc.sbuf_tensor("l1kf1", [128, 128], BF16))
    kf0 = E(nc.sbuf_tensor("l1kf0", [128, 64], BF16))
    fac1 = E(nc.sbuf_tensor("l1fac1", [128, 64], BF16))
    qhat = E(nc.sbuf_tensor("l1qhat", [128, 128], BF16))
    qsub1 = E(nc.sbuf_tensor("l1qsub1", [128, 64], BF16))
    khT = E(nc.sbuf_tensor("l1khT", [128, 128], BF16))
    ktil1 = E(nc.sbuf_tensor("l1ktil1", [128, 128], BF16))
    ktil0 = E(nc.sbuf_tensor("l1ktil0", [128, 64], BF16))
    khn = E(nc.sbuf_tensor("l1khn", [128, 128], BF16))
    vn = E(nc.sbuf_tensor("l1vn", [128, 128], BF16))
    s0m = E(nc.sbuf_tensor("l1s0m", [128, 2 * 64], BF16))
    s1m = E(nc.sbuf_tensor("l1s1m", [128, 2 * 64], BF16))
    qhstg = E(nc.sbuf_tensor("l1qhstg", [128, 2048], BF16))
    psD = E(nc.psum_tensor("l1psD", [128, 3584], F32))
    psDt = E(nc.psum_tensor("l1psDt", [128, 1024], BF16))

    NCH = L1A_NCH
    SCALE = float(1.0 / np.sqrt(64.0))

    with nc.Block() as blk:
        ldD0, mmD0, evD0 = ld.val, mm.val, ev.val
        o2base = ctrs["out2"].val
        STbase = st.val + 1
        EVP, STP, MMP, LDP = 82, 19, 49, 48

        @blk.gpsimd
        def _(g):
            g.memset(zT[:, :], 0.0).then_inc(st.sem, 1)
            st.inc()
            for ct in range(NCT):
                if ct >= 1:
                    g.wait_ge(ev.sem, evD0 + ct * EVP)
                for buf, scr in ((qt, "qT_scr"), (kt, "kT_scr"),
                                 (vt, "vT_scr")):
                    g.dma_start(out=buf[:, :],
                                in_=hbm[scr][:, ct * 2048:(ct + 1) * 2048]
                                ).then_inc(ld.sem, 16)
                    ld.inc(16)
                    g.wait_ge(ld.sem, ld.val)

        @blk.tensor
        def _(t):
            for ct in range(NCT):
                MM0 = mmD0 + ct * MMP
                EV0 = evD0 + ct * EVP
                t.wait_ge(ld.sem, ldD0 + (ct + 1) * LDP)
                if ct >= 1:
                    t.wait_ge(ev.sem, evD0 + ct * EVP)
                ins = None
                for fc in range(4):
                    ins = t.matmul(
                        psD[:, fc * 512:(fc + 1) * 512],
                        tdw2[:, ct * 128:(ct + 1) * 128],
                        twT[:, fc * 512:(fc + 1) * 512],
                        start=True, stop=True)
                ins.then_inc(mm.sem, 1)
                mm.inc()
                for c in range(NCH):
                    cEV0 = EV0 + 2 + c * 5
                    cMM = MM0 + 1 + c * 3
                    t.wait_ge(ev.sem, cEV0 + 2)
                    t.transpose(psDt[:, 0:128], khT[:, :], ident[:, :])
                    ins = t.transpose(psDt[:, 512:640],
                                      vt[:, c * 128:(c + 1) * 128],
                                      ident[:, :])
                    ins.then_inc(mm.sem, 1)
                    mm.inc()
                    t.wait_ge(ev.sem, cEV0 + 3)
                    ins = None
                    for hh in range(2):
                        b = hh * 64
                        tp = (b, 0)
                        t.matmul(psD[0:64, hh * 64:(hh + 1) * 64],
                                 ktil0[b:b + 64, :],
                                 qhat[b:b + 64, 0:64],
                                 start=True, stop=True, tile_position=tp)
                        ins = t.matmul(
                            psD[0:128, 512 + hh * 64:512 + hh * 64 + 64],
                            ktil1[b:b + 64, :], qsub1[b:b + 64, :],
                            start=True, stop=True, tile_position=tp)
                    ins.then_inc(mm.sem, 1)
                    mm.inc()
                    t.wait_ge(ev.sem, cEV0 + 4)
                    ins = None
                    for hh in range(2):
                        b = hh * 64
                        oa = 1024 + hh * 512
                        obk = hh * 512
                        ub = 2048 + hh * 512
                        t.matmul(psD[b:b + 64, oa:oa + 64],
                                 vn[0:64, b:b + 64],
                                 s0m[0:64, hh * 64:(hh + 1) * 64],
                                 start=True, stop=False,
                                 tile_position=(0, b))
                        t.matmul(psD[b:b + 64, obk:obk + 64],
                                 vn[0:128, b:b + 64],
                                 s1m[:, hh * 64:(hh + 1) * 64],
                                 start=True, stop=False,
                                 tile_position=(0, b))
                        t.matmul(psD[b:b + 64, oa:oa + 64],
                                 Sb[b:b + 64, :], qhat[b:b + 64, 0:64],
                                 start=False, stop=True,
                                 tile_position=(b, b))
                        t.matmul(psD[b:b + 64, obk:obk + 64],
                                 Sb[b:b + 64, :], qhat[b:b + 64, 64:128],
                                 start=False, stop=True,
                                 tile_position=(b, b))
                        ins = t.matmul(psD[b:b + 64, ub:ub + 64],
                                       khn[:, b:b + 64], vn[:, b:b + 64],
                                       start=True, stop=True,
                                       tile_position=(0, b))
                    ins.then_inc(mm.sem, 1)
                    mm.inc()

        @blk.scalar
        def _(s):
            for ct in range(NCT):
                MM0 = mmD0 + ct * MMP
                EV0 = evD0 + ct * EVP
                ST0 = STbase + ct * STP
                s.wait_ge(mm.sem, MM0 + 1)
                s.activation(eT[:, :], psD[:, 0:2048], AF.Exp,
                             bias=mcols["td"][:, ct:ct + 1]
                             ).then_inc(st.sem, 1)
                st.inc()
                s.wait_ge(ev.sem, EV0 + 1)
                s.activation(ksT[:, :], eT[:, :], AF.Exp,
                             scale=-1.0).then_inc(st.sem, 1)
                st.inc()
                for c in range(NCH):
                    cEV0 = EV0 + 2 + c * 5
                    ch = slice(c * 128, (c + 1) * 128)
                    s.wait_ge(ev.sem, cEV0 + 1)
                    s.activation(fac[:, :], cumT[:, ch], AF.Exp,
                                 scale=-1.0, bias=colt[:, 0:1])
                    s.activation(kfE[:, :], cumT[:, ch], AF.Exp,
                                 bias=colt[:, 1:2])
                    s.activation(kf1[:, :], cumT[:, ch], AF.Exp,
                                 bias=colt[:, 2:3])
                    s.activation(kf0[:, :], cumT[:, c * 128:c * 128 + 64],
                                 AF.Exp, bias=colt[:, 3:4])
                    s.activation(fac1[:, :],
                                 cumT[:, c * 128 + 64:c * 128 + 128],
                                 AF.Exp, scale=-1.0,
                                 bias=cumT[:, c * 128 + 63:c * 128 + 64])
                    s.activation(colt[:, 6:7], colt[:, 7:8], AF.Exp)
                    ins = s.activation(khT[:, :], cumT[:, ch], AF.Exp,
                                       scale=-1.0)
                    ins.then_inc(st.sem, 1)
                    st.inc()
                s.activation(Gown[:, ct:ct + 1], cumT[:, 2047:2048],
                             AF.Exp, scale=-1.0).then_inc(st.sem, 1)
                st.inc()
                s.wait_ge(ev.sem, EV0 + 2 + 15 * 5 + 2)
                s.dma_start(out=hbm["qh_scr"][:, ct * 2048:(ct + 1) * 2048],
                            in_=qhstg[:, :]).then_inc(ctrs["out2"].sem, 16)
                ctrs["out2"].inc(16)
                s.wait_ge(ctrs["out2"].sem, ctrs["out2"].val)

        @blk.vector
        def _(v):
            for ct in range(NCT):
                MM0 = mmD0 + ct * MMP
                EV0 = evD0 + ct * EVP
                ST0 = STbase + ct * STP
                v.wait_ge(st.sem, ST0 + 1)
                v.tensor_scalar(eT[:, :], eT[:, :], 5.0, None, ALU.min)
                v.tensor_tensor_scan(cumT[:, :], eT[:, :], zT[:, :], 0.0,
                                     ALU.add, ALU.add)
                v.memset(Sfull[:, ct * 64:(ct + 1) * 64], 0.0)
                v.memset(Sb[:, :], 0.0).then_inc(ev.sem, 1)
                ev.inc()
                v.wait_ge(st.sem, ST0 + 2)
                v.tensor_tensor(ksT[:, :], kt[:, :], ksT[:, :], ALU.mult)
                v.tensor_tensor(ksT[:, :], kt[:, :], ksT[:, :],
                                ALU.subtract).then_inc(ev.sem, 1)
                ev.inc()
                for c in range(NCH):
                    cEV0 = EV0 + 2 + c * 5
                    cMM = MM0 + 1 + c * 3
                    cST = ST0 + 2 + c + 1
                    ch = slice(c * 128, (c + 1) * 128)
                    if c == 0:
                        v.memset(colt[:, 0:1], 0.0)
                    else:
                        v.tensor_copy(colt[:, 0:1],
                                      cumT[:, c * 128 - 1:c * 128])
                    v.tensor_scalar(colt[:, 1:2],
                                    cumT[:, c * 128 + 127:c * 128 + 128],
                                    -1.0, None, ALU.mult)
                    v.tensor_scalar(colt[:, 2:3],
                                    cumT[:, c * 128 + 63:c * 128 + 64],
                                    -1.0, None, ALU.mult)
                    v.tensor_scalar(colt[:, 3:4], colt[:, 0:1], -1.0,
                                    None, ALU.mult)
                    v.tensor_tensor(colt[:, 7:8], colt[:, 0:1],
                                    colt[:, 1:2], ALU.add).then_inc(
                                        ev.sem, 1)
                    ev.inc()
                    v.wait_ge(st.sem, cST)
                    v.tensor_tensor(qhat[:, :], qt[:, ch], fac[:, :],
                                    ALU.mult)
                    v.tensor_tensor(qsub1[:, :],
                                    qt[:, c * 128 + 64:c * 128 + 128],
                                    fac1[:, :], ALU.mult)
                    if c == 0 and ct >= 1:
                        v.wait_ge(ctrs["out2"].sem, o2base + ct * 16)
                    v.tensor_tensor(qhstg[:, c * 128:(c + 1) * 128],
                                    qt[:, ch], khT[:, :], ALU.mult)
                    v.tensor_tensor(khT[:, :], ksT[:, ch], kfE[:, :],
                                    ALU.mult)
                    v.tensor_tensor(ktil1[:, :], ksT[:, ch], kf1[:, :],
                                    ALU.mult)
                    v.tensor_tensor(ktil0[:, :],
                                    ksT[:, c * 128:c * 128 + 64],
                                    kf0[:, :], ALU.mult).then_inc(
                                        ev.sem, 1)
                    ev.inc()
                    v.wait_ge(mm.sem, cMM + 1)
                    v.tensor_copy(khn[:, :], psDt[:, 0:128])
                    v.tensor_copy(vn[:, :],
                                  psDt[:, 512:640]).then_inc(ev.sem, 1)
                    ev.inc()
                    v.wait_ge(mm.sem, cMM + 2)
                    ins = None
                    for hh in range(2):
                        v.tensor_tensor(s0m[0:64, hh * 64:(hh + 1) * 64],
                                        psD[0:64, hh * 64:(hh + 1) * 64],
                                        trimask[0:64, :], ALU.mult)
                        v.tensor_copy(
                            s1m[0:64, hh * 64:(hh + 1) * 64],
                            psD[0:64, 512 + hh * 64:512 + hh * 64 + 64])
                        ins = v.tensor_tensor(
                            s1m[64:128, hh * 64:(hh + 1) * 64],
                            psD[64:128, 512 + hh * 64:512 + hh * 64 + 64],
                            trimask[64:128, :], ALU.mult)
                    ins.then_inc(ev.sem, 1)
                    ev.inc()
                    v.wait_ge(mm.sem, cMM + 3)
                    for hh in range(2):
                        b = hh * 64
                        oa = 1024 + hh * 512
                        obk = hh * 512
                        v.scalar_tensor_tensor(
                            Sfull[b:b + 64, ct * 64:(ct + 1) * 64],
                            Sfull[b:b + 64, ct * 64:(ct + 1) * 64],
                            colt[b:b + 64, 6:7],
                            psD[b:b + 64, 2048 + hh * 512:
                                2048 + hh * 512 + 64],
                            ALU.mult, ALU.add)
                        v.tensor_scalar(
                            OT[b:b + 64, ct * 2048 + c * 128:
                               ct * 2048 + c * 128 + 64],
                            psD[b:b + 64, oa:oa + 64], SCALE, None,
                            ALU.mult)
                        v.tensor_scalar(
                            OT[b:b + 64, ct * 2048 + c * 128 + 64:
                               ct * 2048 + (c + 1) * 128],
                            psD[b:b + 64, obk:obk + 64], SCALE, None,
                            ALU.mult)
                    v.tensor_copy(Sb[:, :],
                                  Sfull[:, ct * 64:(ct + 1) * 64]
                                  ).then_inc(ev.sem, 1)
                    ev.inc()

    cS.close()


def _l1attn_cross(nc, cD, sb, ctrs, hbm, OT, umask):
    ld, mm, st, ev = ctrs["ld"], ctrs["mm"], ctrs["st"], ctrs["ev"]
    Sf = sb["_l1Sf"]
    Gown = sb["_l1Gown"]
    cX = contextlib.ExitStack()
    E = cX.enter_context
    Sin = E(nc.sbuf_tensor("l1Sin", [128, 1024], F32))
    SinB = E(nc.sbuf_tensor("l1SinB", [128, 1024], BF16))
    Uj = E(nc.sbuf_tensor("l1Uj", [128, 1024], F32))
    Gj = E(nc.sbuf_tensor("l1Gj", [128, 16], F32))
    qht = E(nc.sbuf_tensor("l1qht", [128, 2 * 2048], BF16))
    psX = E(nc.psum_tensor("l1psX", [128, 1024], F32))
    SCALE = float(1.0 / np.sqrt(64.0))

    with nc.Block() as blk:
        @blk.gpsimd
        def _(g):
            g.dma_start(out=hbm["U_own"][:, :], in_=Sf[:, :]
                        ).then_inc(ld.sem, 16)
            ld.inc(16)
            g.dma_start(out=hbm["G_own"][:, :], in_=Gown[:, :]
                        ).then_inc(ld.sem, 16)
            ld.inc(16)
            g.wait_ge(ld.sem, ld.val)
            g.collective_compute(
                "AllGather", ALU.bypass, replica_groups=GROUPS,
                ins=[hbm["U_own"][:, :]], outs=[hbm["U_g"][:, :]],
            ).then_inc(sb["ccsems"][1].sem, 1)
            g.collective_compute(
                "AllGather", ALU.bypass, replica_groups=GROUPS,
                ins=[hbm["G_own"][:, :]], outs=[hbm["G_g"][:, :]],
            ).then_inc(sb["ccsems"][2].sem, 1)
            g.wait_ge(sb["ccsems"][1].sem, 1)
            g.wait_ge(sb["ccsems"][2].sem, 1)

    with nc.Block() as blk:
        ld0, ev0 = ld.val, ev.val

        @blk.gpsimd
        def _(g):
            g.memset(Sin[:, :], 0.0).then_inc(ev.sem, 1)
            ev.inc()
            for j in range(3):
                if j > 0:
                    g.wait_ge(ev.sem, ev0 + 1 + j)
                g.dma_start(out=Uj[:, :],
                            in_=hbm["U_g"][j * 128:(j + 1) * 128, :]
                            ).then_inc(ld.sem, 16)
                ld.inc(16)
                g.dma_start(out=Gj[:, :],
                            in_=hbm["G_g"][j * 128:(j + 1) * 128, :]
                            ).then_inc(ld.sem, 16)
                ld.inc(16)
                g.wait_ge(ld.sem, ld.val)

        @blk.vector
        def _(v):
            v.wait_ge(ev.sem, ev0 + 1)
            for j in range(3):
                v.wait_ge(ld.sem, ld0 + (j + 1) * 32)
                v.tensor_scalar(Gj[:, :], Gj[:, :], umask[:, j:j + 1],
                                sb["umask2"][:, j:j + 1], ALU.mult,
                                ALU.add)
                v.tensor_scalar(Uj[:, :], Uj[:, :], umask[:, j:j + 1],
                                None, ALU.mult)
                ins = None
                for ct in range(NCT):
                    ins = v.scalar_tensor_tensor(
                        Sin[:, ct * 64:(ct + 1) * 64],
                        Sin[:, ct * 64:(ct + 1) * 64],
                        Gj[:, ct:ct + 1],
                        Uj[:, ct * 64:(ct + 1) * 64],
                        ALU.mult, ALU.add)
                ins.then_inc(ev.sem, 1)
                ev.inc()
            v.tensor_copy(SinB[:, :], Sin[:, :]).then_inc(ev.sem, 1)
            ev.inc()

    with nc.Block() as blk:
        ld0, mm0, ev0 = ld.val, mm.val, ev.val

        @blk.gpsimd
        def _(g):
            for ct in range(NCT):
                buf = ct % 2
                if ct >= 2:
                    g.wait_ge(mm.sem, mm0 + (ct - 1) * 4)
                g.dma_start(out=qht[:, buf * 2048:(buf + 1) * 2048],
                            in_=hbm["qh_scr"][:, ct * 2048:(ct + 1) * 2048]
                            ).then_inc(ld.sem, 16)
                ld.inc(16)
                g.wait_ge(ld.sem, ld.val)

        @blk.tensor
        def _(t):
            for ct in range(NCT):
                buf = ct % 2
                t.wait_ge(ld.sem, ld0 + (ct + 1) * 16)
                for fc in range(4):
                    i = ct * 4 + fc
                    if i >= 2:
                        t.wait_ge(ev.sem, ev0 + i - 1)
                    ins = None
                    for hh in range(2):
                        b = hh * 64
                        ins = t.matmul(
                            psX[b:b + 64, (i % 2) * 512:(i % 2) * 512 + 512],
                            SinB[b:b + 64, ct * 64:(ct + 1) * 64],
                            qht[b:b + 64, buf * 2048 + fc * 512:
                                buf * 2048 + (fc + 1) * 512],
                            start=True, stop=True, tile_position=(b, b))
                    ins.then_inc(mm.sem, 1)
                    mm.inc()

        @blk.vector
        def _(v):
            for ct in range(NCT):
                for fc in range(4):
                    i = ct * 4 + fc
                    v.wait_ge(mm.sem, mm0 + i + 1)
                    sl = slice(ct * 2048 + fc * 512,
                               ct * 2048 + (fc + 1) * 512)
                    ins = None
                    for hh in range(2):
                        b = hh * 64
                        ins = v.scalar_tensor_tensor(
                            OT[b:b + 64, sl],
                            psX[b:b + 64, (i % 2) * 512:(i % 2) * 512 + 512],
                            SCALE, OT[b:b + 64, sl], ALU.mult, ALU.add)
                    ins.then_inc(ev.sem, 1)
                    ev.inc()

    cX.close()


def build_l1attn_launch():
    nc = bass.Bass("TRN2", target_bir_lowering=False)
    hbm = {}
    hbm["xT"] = nc.dram_tensor("xT", [C, T], F32, kind="ExternalInput").ap()
    for name, shape in [("q_w", [C, C]), ("k_w", [C, C]), ("v_w", [C, C]),
                        ("o_w", [C, C]), ("maa_w1", [C, 128]),
                        ("maa_w2", [4, 32, C]), ("td_w1", [C, 64]),
                        ("td_w2", [64, C]), ("prevsel", [4, 1]),
                        ("umask", [128, 3]), ("umask2", [128, 3])]:
        hbm[name] = nc.dram_tensor(name, shape, F32,
                                   kind="ExternalInput").ap()
    for nm in ("maa_x", "maa_r", "maa_k", "maa_v", "maa_w", "q_b", "k_b",
               "v_b", "td", "ln_g", "ln_b"):
        hbm["col_" + nm] = nc.dram_tensor("col_" + nm, [128, 16], F32,
                                          kind="ExternalInput").ap()
    hbm["yT"] = nc.dram_tensor("yT", [C, T], F32, kind="ExternalOutput").ap()
    for name, shape, dt in [("qw_scr", [C, C], BF16),
                            ("kw_scr", [C, C], BF16),
                            ("vw_scr", [C, C], BF16),
                            ("ow_scr", [C, C], BF16),
                            ("u_scr", [128, NCT * T], BF16),
                            ("qT_scr", [128, NCT * T], BF16),
                            ("kT_scr", [128, NCT * T], BF16),
                            ("vT_scr", [128, NCT * T], BF16),
                            ("qh_scr", [128, NCT * T], BF16),
                            ("bnd_own", [1, C], F32),
                            ("bnd_g", [4, C], F32),
                            ("bprev_scr", [1, C], F32),
                            ("U_own", [128, 1024], F32),
                            ("U_g", [512, 1024], F32),
                            ("G_own", [128, 16], F32),
                            ("G_g", [512, 16], F32)]:
        hbm[name] = nc.dram_tensor(name, shape, dt).ap()

    with contextlib.ExitStack() as ctx:
        E = ctx.enter_context
        sb = {}
        sb["ones128"] = E(nc.sbuf_tensor("ones128", [128, 128], BF16))
        sb["eps_ap"] = E(nc.sbuf_tensor("eps_ap", [128, 1], F32))
        sb["halfpi"] = E(nc.sbuf_tensor("halfpi", [128, 1], F32))
        sb["mb"] = E(nc.sbuf_tensor("s_mbuf", [128, 2048], F32))
        sb["rb"] = E(nc.sbuf_tensor("s_rbuf", [128, 2048], F32))
        sb["sqr2"] = E(nc.sbuf_tensor("s_sqr2", [128, 2048], BF16))
        sb["umask2"] = E(nc.sbuf_tensor("s_umask2", [128, 3], F32))
        ctrs = {}
        for name in ["ld", "lds", "lda", "wconv", "wld", "mm", "sq", "st",
                     "ev", "out", "out2", "cc"]:
            ctrs[name] = Ctr(E(nc.semaphore("s_" + name)))
        sb["ccsems"] = [Ctr(E(nc.semaphore(f"s_cc{k}"))) for k in range(3)]

        with nc.Block() as blk:
            @blk.gpsimd
            def _(g):
                g.memset(sb["eps_ap"][:, :], LN_EPS)
                g.memset(sb["halfpi"][:, :], PI / 2)
                g.memset(sb["ones128"][:, :], 1.0)
                g.dma_start(out=sb["umask2"][:, :], in_=hbm["umask2"][:, :]
                            ).then_inc(ctrs["ld"].sem, 16)
                ctrs["ld"].inc(16)
                g.wait_ge(ctrs["ld"].sem, ctrs["ld"].val)

        with contextlib.ExitStack() as ctx2:
            emit_l1attn_sublayer(nc, ctx2, sb, ctrs, hbm, hbm["xT"],
                                 hbm["yT"])
    return nc


def l1attn_in_maps(inp, shards_xT):
    g = lambda k: np.ascontiguousarray(np.asarray(inp[k]))
    base = {
        "q_w": g("L1_attn_q_w"), "k_w": g("L1_attn_k_w"),
        "v_w": g("L1_attn_v_w"), "o_w": g("L1_attn_o_w"),
        "maa_w1": g("L1_attn_maa_w1"), "maa_w2": g("L1_attn_maa_w2"),
        "td_w1": g("L1_attn_td_w1"), "td_w2": g("L1_attn_td_w2"),
    }
    for nm, key in (("maa_x", "L1_attn_maa_x"), ("maa_r", "L1_attn_maa_r"),
                    ("maa_k", "L1_attn_maa_k"), ("maa_v", "L1_attn_maa_v"),
                    ("maa_w", "L1_attn_maa_w"), ("q_b", "L1_attn_q_b"),
                    ("k_b", "L1_attn_k_b"), ("v_b", "L1_attn_v_b"),
                    ("td", "L1_attn_time_decay"), ("ln_g", "L1_attn_ln_g"),
                    ("ln_b", "L1_attn_ln_b")):
        base["col_" + nm] = _pcol(inp[key], 16)
    maps = []
    for c in range(NCORES):
        m = dict(base)
        m["xT"] = shards_xT[c]
        gp = c % 4
        sel = np.zeros((4, 1), np.float32)
        if gp > 0:
            sel[gp - 1, 0] = 1.0
        m["prevsel"] = sel
        um = np.zeros((128, 3), np.float32)
        um[:, :max(gp, 0)] = 1.0 if gp > 0 else 0.0
        for j in range(3):
            um[:, j] = 1.0 if j < gp else 0.0
        m["umask"] = np.ascontiguousarray(um)
        m["umask2"] = np.ascontiguousarray(1.0 - um)
        maps.append(m)
    return maps


def run_l1attn(x, inp):
    nc = build_l1attn_launch()
    maps = l1attn_in_maps(inp, _shard_xT(x))
    res = run_bass_kernel_spmd(nc, maps, core_ids=list(range(NCORES)))
    return _unshard_xT([res.results[c]["yT"] for c in range(NCORES)])



def _rope_perm():
    perm = np.zeros(2048, dtype=np.int64)
    for h in range(NH):
        perm[h * 128:h * 128 + 64] = h * 128 + np.arange(0, 128, 2)
        perm[h * 128 + 64:(h + 1) * 128] = h * 128 + np.arange(1, 128, 2)
    return perm


def build_l0attn_launch(upto=99):
    nc = bass.Bass("TRN2", target_bir_lowering=False)
    hbm = {}
    hbm["xT"] = nc.dram_tensor("xT", [C, T], F32, kind="ExternalInput").ap()
    for name in ("wqTp", "wkTp", "wvT", "woT"):
        hbm[name] = nc.dram_tensor(name, [C, C], F32,
                                   kind="ExternalInput").ap()
    hbm["posrow"] = nc.dram_tensor("posrow", [1, 128], F32,
                                   kind="ExternalInput").ap()
    hbm["invfreq"] = nc.dram_tensor("invfreq", [1, 64], F32,
                                    kind="ExternalInput").ap()
    hbm["yT"] = nc.dram_tensor("yT", [C, T], F32, kind="ExternalOutput").ap()
    for name, shape, dt in [("wqT_scr", [C, C], BF16),
                            ("wkT_scr", [C, C], BF16),
                            ("wvT_scr", [C, C], BF16),
                            ("woT_scr", [C, C], BF16),
                            ("u_scr", [128, NCT * T], BF16),
                            ("kT_own", [NH * 128, T], BF16),
                            ("v_own", [T, C], BF16),
                            ("kn_own", [1, NH], F32),
                            ("kT_g", [4 * NH * 128, T], BF16),
                            ("v_g", [4 * T, C], BF16),
                            ("kn_g", [4, NH], F32)]:
        hbm[name] = nc.dram_tensor(name, shape, dt).ap()

    with contextlib.ExitStack() as ctx:
        E = ctx.enter_context
        sb = {}
        sb["ones128"] = E(nc.sbuf_tensor("ones128", [128, 128], BF16))
        sb["ones1f"] = E(nc.sbuf_tensor("ones1f", [1, 128], F32))
        sb["eps_ap"] = E(nc.sbuf_tensor("eps_ap", [128, 1], F32))
        sb["halfpi"] = E(nc.sbuf_tensor("halfpi", [128, 1], F32))
        ctrs = {}
        for name in ["ld", "lds", "lda", "wconv", "wld", "mm", "sq", "st",
                     "ev", "out", "out2", "cc"]:
            ctrs[name] = Ctr(E(nc.semaphore("s_" + name)))
        sb["ccsems"] = [Ctr(E(nc.semaphore(f"s_cc{k}"))) for k in range(3)]

        with nc.Block() as blk:
            @blk.gpsimd
            def _(g):
                g.memset(sb["eps_ap"][:, :], LN_EPS)
                g.memset(sb["halfpi"][:, :], PI / 2)
                g.memset(sb["ones128"][:, :], 1.0)
                g.memset(sb["ones1f"][:, :], 1.0)
                wc = ctrs["wconv"]
                for a, b in (("wqTp", "wqT_scr"), ("wkTp", "wkT_scr"),
                             ("wvT", "wvT_scr"), ("woT", "woT_scr")):
                    g.dma_start(out=hbm[b][:, :], in_=hbm[a][:, :]
                                ).then_inc(wc.sem, 16)
                    wc.inc(16)
                g.wait_ge(wc.sem, wc.val)

        with contextlib.ExitStack() as ctx2:
            emit_l0attn_sublayer(nc, ctx2, sb, ctrs, hbm, hbm["xT"],
                                 hbm["yT"], upto=upto)
        if upto < 99:
            with nc.Block() as blk:
                @blk.gpsimd
                def _(g):
                    o = ctrs["out"]
                    g.dma_start(out=hbm["yT"][:, :], in_=hbm["xT"][:, :]
                                ).then_inc(o.sem, 16)
                    o.inc(16)
                    g.wait_ge(o.sem, o.val)
    return nc


def l0attn_in_maps(inp, shards_xT):
    wq = np.asarray(inp["L0_attn_wq"])
    wk = np.asarray(inp["L0_attn_wk"])
    wv = np.asarray(inp["L0_attn_wv"])
    wo = np.asarray(inp["L0_attn_wo"])
    perm = _rope_perm()
    wqTp = np.ascontiguousarray(wq.T[:, perm])
    wkTp = np.ascontiguousarray(wk.T[:, perm])
    wvT = np.ascontiguousarray(wv.T)
    woT = np.ascontiguousarray(wo.T)
    invf = np.ascontiguousarray(np.asarray(inp["L0_attn_inv_freq"]
                                           ).reshape(1, 64))
    maps = []
    for c in range(NCORES):
        t0 = (c % 4) * T
        posrow = (t0 + np.arange(128, dtype=np.float32)).reshape(1, 128)
        maps.append({"xT": shards_xT[c], "wqTp": wqTp, "wkTp": wkTp,
                     "wvT": wvT, "woT": woT,
                     "posrow": np.ascontiguousarray(posrow),
                     "invfreq": invf})
    return maps


def run_l0attn(x, inp):
    nc = build_l0attn_launch()
    maps = l0attn_in_maps(inp, _shard_xT(x))
    res = run_bass_kernel_spmd(nc, maps, core_ids=list(range(NCORES)))
    return _unshard_xT([res.results[c]["yT"] for c in range(NCORES)])
